# version 08
# baseline (speedup 1.0000x reference)
import numpy as np
import ml_dtypes
import concourse.bass as bass
import concourse.mybir as mybir
from concourse.ap import AP
from concourse.bass_utils import run_bass_kernel_spmd

F32 = mybir.dt.float32
BF16 = mybir.dt.bfloat16
AF = mybir.ActivationFunctionType
ALU = mybir.AluOpType
bf = ml_dtypes.bfloat16

BC, L, D = 2, 2048, 1024
H, DH, DS, DI = 16, 64, 48, 2048
NL, NT = H * DS, 6
DH_SUB, TC = 4, 512
NDHG, NTC = DH // DH_SUB, L // TC
CC = DH_SUB * TC
TA, NSUB = 512, L // 512
DT_BLK, DU_BLK = 128 * TC, 64 * TC
RND = 12582912.0
DBG = False
LN_EPS, RMS_EPS = 1e-5, 1e-6

LANE_H = np.array([l % 16 for l in range(128)])
ENGINES = ("q", "t", "a", "v", "g")
_ATTR = {"q": "sync", "t": "tensor", "a": "scalar", "v": "vector",
         "g": "gpsimd"}


class Sched:
    def __init__(self):
        self.ops = []
        self.cnt = {e: 0 for e in ENGINES}

    def op(self, eng, fn, waits=(), inc=1):
        ws = []
        for w in waits:
            if w is None:
                continue
            if isinstance(w, (list, tuple)) and w and isinstance(w[0], tuple):
                ws.extend(x for x in w if x is not None)
            else:
                ws.append(w)
        for (we, wv) in ws:
            if we == eng:
                assert wv <= self.cnt[eng], "self-wait on future op"
        self.ops.append((eng, fn, inc, ws))
        self.cnt[eng] += inc
        return (eng, self.cnt[eng])

    def dma(self, eng, fn, waits=()):
        return self.op(eng, fn, waits, inc=16)

    def render(self, block, sems):
        wvals = {e: set() for e in ENGINES}
        for (_, _, _, waits) in self.ops:
            for (we, wv) in waits:
                wvals[we].add(wv)
        for eng in ENGINES:
            bvals = sorted(wvals[eng]) if eng in ("q", "g") else []

            def body(o, eng=eng, bvals=bvals):
                waited = {e: 0 for e in ENGINES}
                cum = 0
                bi = 0
                selfw = 0
                for (e, fn, inc, waits) in self.ops:
                    if e != eng:
                        continue
                    while bi < len(bvals) and bvals[bi] <= cum:
                        bi += 1
                    if bi > 0 and bvals[bi - 1] > selfw:
                        o.wait_ge(sems[eng], bvals[bi - 1])
                        selfw = bvals[bi - 1]
                    for (we, wv) in waits:
                        if wv > waited[we]:
                            o.wait_ge(sems[we], wv)
                            waited[we] = wv
                    fn(o).then_inc(sems[eng], inc)
                    cum += inc
                if eng == "q":
                    for e2 in ENGINES:
                        if e2 != "q" and self.cnt[e2] > waited[e2]:
                            o.wait_ge(sems[e2], self.cnt[e2])
                    o.wait_ge(sems["q"], self.cnt["q"])
            getattr(block, _ATTR[eng])(body)


def host_tables():
    pat_dt = np.zeros((DH_SUB, 64, 128), np.float32)
    pat_du = np.zeros((DH_SUB, 32, 128), np.float32)
    for d in range(DH_SUB):
        p_ = d // 2
        for k in range(64):
            gg = p_ * 64 + k
            if gg // 32 == d:
                pat_dt[d, k, :] = (LANE_H == (gg % 32) // 2)
        for k in range(32):
            gg = p_ * 32 + k
            if gg // 16 == d:
                pat_du[d, k, :] = (LANE_H == gg % 16)
    yd = np.zeros((128, 16), np.float32)
    for ml in range(128):
        yd[ml, LANE_H[ml]] = 1.0
    pat_dt = pat_dt.transpose(1, 0, 2).reshape(64, DH_SUB * 128)
    pat_du = pat_du.transpose(1, 0, 2).reshape(32, DH_SUB * 128)
    return pat_dt.astype(bf), pat_du.astype(bf), yd.astype(bf)


def build():
    nc = bass.Bass("TRN2", target_bir_lowering=False)
    S = Sched()

    ein = lambda n, s, dt=F32: nc.dram_tensor(n, s, dt, kind="ExternalInput").ap()
    x_in = ein("x", [BC, L, D])
    norm1_w = ein("norm1_w", [D])
    norm2_w = ein("norm2_w", [D])
    dw_kernel = ein("dw_kernel", [D, 1, 4])
    dw_bias = ein("dw_bias", [D])
    pw_w = ein("pw_w", [D, D])
    pw_b = ein("pw_b", [D])
    xp_w = ein("xp_w", [160, DH])
    dtp_w = ein("dtp_w", [DH, 64])
    dtp_b = ein("dtp_b", [64])
    out_w = ein("out_w", [D, D])
    out_b = ein("out_b", [D])
    gate_w = ein("gate_w", [DI, D])
    up_w = ein("up_w", [DI, D])
    down_w = ein("down_w", [D, DI])
    ffn_w = ein("ffn_w", [D, D])
    ffn_b = ein("ffn_b", [D])
    t_pdt = ein("t_pdt", [64, DH_SUB * 128], BF16)
    t_pdu = ein("t_pdu", [32, DH_SUB * 128], BF16)
    t_yd = ein("t_yd", [128, 16], BF16)
    t_alog = ein("t_alog", [128, NT])
    t_idb = ein("t_idb", [128, 128], BF16)
    t_idf = ein("t_idf", [128, 128])

    out_t = nc.dram_tensor("out", [BC, L, D], F32, kind="ExternalOutput").ap()

    dint = lambda n, s, dt: nc.dram_tensor(n, s, dt, kind="Internal").ap()
    g_dt = dint("g_dt", [BC, NDHG, NTC, DT_BLK], BF16)
    g_du = dint("g_du", [BC, NDHG, NTC, DU_BLK], BF16)
    g_bt = dint("g_bt", [BC, NL, L], BF16)
    g_ct = dint("g_ct", [BC, NL, L], BF16)
    g_y = dint("g_y", [BC, D, L], BF16)
    g_xn = dint("g_xn", [BC, D, L], F32)
    g_xq = dint("g_xq", [BC, D, L], BF16)
    g_xsc = dint("g_xsc", [BC, L], F32)
    g_x1 = dint("g_x1", [BC, D, L], F32)
    g_h2 = dint("g_h2", [BC, D, L], BF16)
    g_mid = dint("g_mid", [BC, DI, L], BF16)
    g_pwT = dint("g_pwT", [D, D], BF16)
    g_opT = dint("g_opT", [D, D], BF16)
    g_gwT = dint("g_gwT", [D, DI], BF16)
    g_uwT = dint("g_uwT", [D, DI], BF16)
    g_dnT = dint("g_dnT", [DI, D], BF16)
    g_fqT = dint("g_fqT", [D, D], BF16)
    if DBG:
        g_dbg_s = dint("g_dbg_s", [BC, D, L], F32)
        g_dbg_g = dint("g_dbg_g", [BC, D, L], F32)

    PF_N, PB_N = 27, 40
    pf = nc.alloc_sbuf_tensor("pf", [128, PF_N * 1024], F32)
    pb = nc.alloc_sbuf_tensor("pb", [128, PB_N * 1024], BF16)

    def PF(i, c0=0, n=1024, r0=0, rows=128):
        return AP(pf, r0 * (PF_N * 1024) + i * 1024 + c0,
                  [[PF_N * 1024, rows], [1, n]])

    def PB(i, c0=0, n=1024, r0=0, rows=128):
        return AP(pb, r0 * (PB_N * 1024) + i * 1024 + c0,
                  [[PB_N * 1024, rows], [1, n]])

    sbt = lambda n, s, dt=F32: nc.alloc_sbuf_tensor(n, s, dt)
    idb_s = sbt("idb", [128, 128], BF16)
    idf_s = sbt("idf", [128, 128])
    colA = sbt("colA", [128, 15])
    colB = sbt("colB", [128, 9])
    kcol = sbt("kcol", [128, 32])
    c_pwb = sbt("c_pwb", [128, 8])
    c_n1 = sbt("c_n1", [128, 8])
    c_n2 = sbt("c_n2", [128, 8])
    c_ob = sbt("c_ob", [128, 8])
    c_fb = sbt("c_fb", [128, 8])
    c_ws = sbt("c_ws", [128, 8])
    c_dtb = sbt("c_dtb", [128, 1])
    c_dwb = sbt("c_dwb", [128, 8])
    xwT_s = sbt("xwT", [128, 160], BF16)
    dwT_s = sbt("dwT", [128, 64], BF16)
    s_dtx = sbt("s_dtx", [128, 8 * TA], BF16)
    s_bc = sbt("s_bc", [48, 2 * TA], BF16)
    s_acol = sbt("s_acol", [128, NT])
    s_pd = sbt("s_pd", [128, DH_SUB * 128], BF16)
    s_yd = sbt("s_yd", [128, 16], BF16)
    s_rd = sbt("s_rd", [128, 2 * TC], BF16)
    s_carry = sbt("s_carry", [128, NT * DH_SUB])
    s_tmp = sbt("s_tmp", [128, DH_SUB])
    s_yev = sbt("s_yev", [16, 2048], BF16)
    halo_b = sbt("halo", [128, 24])
    prepQ = sbt("prepQ", [128, 4])
    ones_c = sbt("ones_c", [128, 128], BF16)
    e0_s = sbt("e0_s", [128, 128], BF16)

    ps_mm = nc.alloc_psum_tensor("ps_mm", [128, 2048], F32)
    ps_t = nc.alloc_psum_tensor("ps_t", [128, 512], F32)
    ps_r = nc.alloc_psum_tensor("ps_r", [128, 512], F32)
    ps_y = nc.alloc_psum_tensor("ps_y", [128, 512], F32)
    ps_tb = nc.alloc_psum_tensor("ps_tb", [128, 512], BF16)

    def dma(dst, src, waits=(), slow=False):
        if slow:
            return S.dma("q", lambda o, d=dst, s=src: o.dma_start(
                out=d, in_=s, allow_slow_non_contiguous=True), waits)
        return S.dma("q", lambda o, d=dst, s=src: o.dma_start(out=d, in_=s),
                     waits)

    def gdma(dst, src, waits=()):
        return S.dma("g", lambda o, d=dst, s=src: o.dma_start(out=d, in_=s),
                     waits)

    def vcopy(dst, src, waits=()):
        return S.op("v", lambda o, d=dst, s=src: o.tensor_copy(out=d, in_=s),
                    waits)

    def vtt(dst, a0, a1, op, waits=()):
        return S.op("v", lambda o, d=dst, x=a0, y=a1, op=op:
                    o.tensor_tensor(out=d, in0=x, in1=y, op=op), waits)

    def gtt(dst, a0, a1, op, waits=()):
        return S.op("g", lambda o, d=dst, x=a0, y=a1, op=op:
                    o.tensor_tensor(out=d, in0=x, in1=y, op=op), waits)

    def vts(dst, src, s1, s2=None, op0=ALU.mult, op1=None, waits=()):
        if s2 is None:
            return S.op("v", lambda o, d=dst, s=src, s1=s1, op0=op0:
                        o.tensor_scalar(out=d, in0=s, scalar1=s1,
                                        scalar2=None, op0=op0), waits)
        return S.op("v", lambda o, d=dst, s=src, s1=s1, s2=s2, op0=op0,
                    op1=op1: o.tensor_scalar(out=d, in0=s, scalar1=s1,
                                             scalar2=s2, op0=op0, op1=op1),
                    waits)

    def vstt(dst, in0, scal, in1, op0, op1, waits=()):
        return S.op("v", lambda o, d=dst, i0=in0, sc=scal, i1=in1, op0=op0,
                    op1=op1: o.scalar_tensor_tensor(
                        out=d, in0=i0, scalar=sc, in1=i1, op0=op0, op1=op1),
                    waits)

    def vred(dst, src, op=ALU.add, absv=False, waits=()):
        if op == ALU.add:
            return S.op("v", lambda o, d=dst, s=src, ab=absv: o.reduce_sum(
                out=d, in_=s, axis=mybir.AxisListType.X,
                apply_absolute_value=ab), waits)
        return S.op("v", lambda o, d=dst, s=src, ab=absv: o.reduce_max(
            out=d, in_=s, axis=mybir.AxisListType.X,
            apply_absolute_value=ab), waits)

    def vrecip(dst, src, waits=()):
        return S.op("v", lambda o, d=dst, s=src: o.reciprocal(out=d, in_=s),
                    waits)

    def vmemset(dst, val, waits=()):
        return S.op("v", lambda o, d=dst, vv=val: o.memset(d, vv), waits)

    def vscan(dst, d0, d1, waits=()):
        return S.op("v", lambda o, d=dst, a0=d0, a1=d1: o.tensor_tensor_scan(
            out=d, data0=a0, data1=a1, initial=0.0, op0=ALU.mult,
            op1=ALU.add), waits)

    def act(dst, src, func, bias=0.0, scale=1.0, waits=()):
        return S.op("a", lambda o, d=dst, s=src, f=func, b=bias, sc=scale:
                    o.activation(out=d, in_=s, func=f, bias=b, scale=sc),
                    waits)

    def mm(out, lhsT, rhs, start, stop, waits=(), skip=False):
        return S.op("t", lambda o, ot=out, lh=lhsT, rh=rhs, st=start,
                    sp=stop, sk=skip: o.matmul(out=ot, lhsT=lh, rhs=rh,
                                               start=st, stop=sp,
                                               skip_group_check=sk), waits)

    def tT(out, src, ident, waits=()):
        return S.op("t", lambda o, ot=out, s=src, idn=ident:
                    o.transpose(out=ot, in_=s, identity=idn), waits)

    class Banks:
        def __init__(self):
            self.last = [None] * 4
            self.i = 0

        def get(self):
            b_ = self.i
            self.i = (b_ + 1) % 4
            return b_, ps_mm.ap()[:, b_ * 512:(b_ + 1) * 512], self.last[b_]

        def done(self, b_, m):
            self.last[b_] = m

        def drain(self):
            return [m for m in self.last if m is not None]

    BK = Banks()

    def load_dcol(dst_t, src_vec, waits=()):
        return dma(AP(dst_t, 0, [[dst_t.shape[1], 128], [1, 8]]),
                   AP(src_vec.tensor, 0, [[1, 128], [128, 8]]),
                   waits=waits, slow=True)

    m_idb = dma(idb_s.ap(), t_idb)
    m_idf = dma(idf_s.ap(), t_idf)
    m_pdt = dma(AP(s_pd, 0, [[DH_SUB * 128, 64], [1, DH_SUB * 128]]), t_pdt)
    m_pdu = dma(AP(s_pd, 64 * DH_SUB * 128, [[DH_SUB * 128, 32],
                                             [1, DH_SUB * 128]]), t_pdu)
    m_yd = dma(s_yd.ap(), t_yd)
    m_al = dma(s_acol.ap(), t_alog)
    m_ax = act(s_acol.ap(), s_acol.ap(), AF.Exp, waits=[m_al])
    m_acol = vts(s_acol.ap(), s_acol.ap(), -1.0, waits=[m_ax])
    m_on1 = vmemset(ones_c.ap(), 1.0)
    m_e0a = vmemset(e0_s.ap(), 0.0)
    m_on2 = vmemset(e0_s.ap()[0:1, :], 1.0, waits=[m_e0a])

    m_n1 = load_dcol(c_n1, norm1_w)
    m_n2 = load_dcol(c_n2, norm2_w)
    m_ob = load_dcol(c_ob, out_b)
    m_fb = load_dcol(c_fb, ffn_b)
    m_dwb = load_dcol(c_dwb, dw_bias)
    m_pbv = load_dcol(colB, pw_b)
    m_dtb = dma(AP(c_dtb, 0, [[1, 64], [1, 1]]),
                AP(dtp_b.tensor, 0, [[1, 64], [1, 1]]))
    m_kc = dma(AP(kcol, 0, [[32, 128], [4, 8], [1, 4]]),
               AP(dw_kernel.tensor, 0, [[4, 128], [512, 8], [1, 4]]))
    m_kf = None
    for i in range(8):
        m_kf = vts(kcol.ap()[:, 4 * i:4 * i + 4],
                   kcol.ap()[:, 4 * i:4 * i + 4],
                   c_n1.ap()[:, i:i + 1], waits=[m_kc, m_n1, m_kf])

    ps_tb_last = [None]
    pb_ev_last = [None, None]

    def prep_T(w_ap, gdst, O, K, quant=False, ws_col=None):
        mstore, mload = None, None
        for ot in range(O // 128):
            md = dma(PF(0, 0, K), w_ap[ot * 128:(ot + 1) * 128, 0:K],
                     waits=[mload])
            src = md
            if quant:
                m1 = vred(prepQ.ap()[:, 0:1], PF(0, 0, K), op=ALU.max,
                          absv=True, waits=[md])
                m2 = vrecip(prepQ.ap()[:, 1:2], prepQ.ap()[:, 0:1],
                            waits=[m1])
                m3 = vts(prepQ.ap()[:, 2:3], prepQ.ap()[:, 1:2], 127.0,
                         waits=[m2])
                S.op("v", lambda o, ot=ot, ws=ws_col: o.tensor_scalar(
                    out=ws.ap()[:, ot:ot + 1], in0=prepQ.ap()[:, 0:1],
                    scalar1=1.0 / 127.0, scalar2=None, op0=ALU.mult),
                    waits=[m1])
                m5 = vts(PF(0, 0, K), PF(0, 0, K), prepQ.ap()[:, 2:3],
                         waits=[m3])
                src = vts(PF(0, 0, K), PF(0, 0, K), RND, RND, ALU.add,
                          ALU.subtract, waits=[m5])
            mc = vcopy(PB(0, 0, K), PF(0, 0, K), waits=[src])
            mload = mc
            for g4 in range(K // 512):
                mts_ = []
                for j in range(4):
                    kt = g4 * 4 + j
                    mts_.append(tT(ps_tb.ap()[:, j * 128:(j + 1) * 128],
                                   PB(0, kt * 128, 128), idb_s.ap(),
                                   waits=[mc, m_idb,
                                          mts_[-1] if mts_
                                          else ps_tb_last[0]]))
                ev = g4 % 2
                me = act(PB(0, K + ev * 512, 512), ps_tb.ap()[:, 0:512],
                         AF.Copy, waits=[mts_[-1], pb_ev_last[ev]])
                ps_tb_last[0] = me
                for j in range(4):
                    kt = g4 * 4 + j
                    mstore = dma(AP(gdst.tensor,
                                    gdst.offset + kt * 128 * O + ot * 128,
                                    [[O, 128], [1, 128]]),
                                 PB(0, K + ev * 512 + j * 128, 128),
                                 waits=[me] if j == 0 else [])
                pb_ev_last[ev] = mstore
        return mstore

    m_pw = prep_T(pw_w, g_pwT, D, D)
    m_opw = prep_T(out_w, g_opT, D, D)
    m_gw = prep_T(gate_w, g_gwT, DI, D)
    m_uw = prep_T(up_w, g_uwT, DI, D)
    m_dn = prep_T(down_w, g_dnT, D, DI)
    m_fq = prep_T(ffn_w, g_fqT, D, D, quant=True, ws_col=c_ws)

    m0 = dma(PF(0, 0, 64), xp_w[0:128, :], waits=[m_fq])
    m1 = dma(PF(1, 0, 64, rows=32), xp_w[128:160, :])
    m2 = vcopy(PB(0, 0, 64), PF(0, 0, 64), waits=[m0, m_fq])
    m3 = vcopy(PB(1, 0, 64, rows=32), PF(1, 0, 64, rows=32), waits=[m1])
    m4 = tT(ps_tb.ap()[0:64, 0:128], PB(0, 0, 64), idb_s.ap(),
            waits=[m2, ps_tb_last[0]])
    m5 = act(AP(xwT_s, 0, [[160, 64], [1, 128]]), ps_tb.ap()[0:64, 0:128], AF.Copy,
             waits=[m4])
    m6 = tT(ps_tb.ap()[0:64, 128:160], PB(1, 0, 64, rows=32),
            idb_s.ap()[0:32, 0:32], waits=[m3, m5, ps_tb_last[0]])
    m7 = act(AP(xwT_s, 128, [[160, 64], [1, 32]]), ps_tb.ap()[0:64, 128:160], AF.Copy,
             waits=[m6])
    m8 = dma(PF(0, 0, 64, rows=64), dtp_w, waits=[m2])
    m9 = vcopy(PB(0, 0, 64, rows=64), PF(0, 0, 64, rows=64),
               waits=[m8, m3, m4])
    m10 = tT(ps_tb.ap()[0:64, 256:320], PB(0, 0, 64, rows=64),
             idb_s.ap()[0:64, 0:64], waits=[m9, m7, ps_tb_last[0]])
    m_dwT0 = act(AP(dwT_s, 0, [[64, 64], [1, 64]]),
                 ps_tb.ap()[0:64, 256:320], AF.Copy, waits=[m10])
    m_xwC = vcopy(AP(xwT_s, 64 * 160, [[160, 64], [1, 160]]),
                  AP(xwT_s, 0, [[160, 64], [1, 160]]), waits=[m5, m7])
    m_dwT = vcopy(AP(dwT_s, 64 * 64, [[64, 64], [1, 64]]),
                  AP(dwT_s, 0, [[64, 64], [1, 64]]), waits=[m_dwT0, m_xwC])
    ps_tb_last[0] = m_dwT0

    def load_pwT(waits):
        ms = []
        for kt in range(8):
            ms.append(dma(PB(29 + kt, 0, 1024),
                          AP(g_pwT.tensor, kt * 128 * D,
                             [[D, 128], [1, 1024]]), waits=waits))
        return ms

    m_pwT = load_pwT([m_pw, m_dwT])
    m_dwbc = vcopy(PB(28, 0, 8), c_dwb.ap()[:, 0:8], waits=[m_dwb, m9])
    m_bias = None
    for ot in range(8):
        mmsl = None
        for kt in range(8):
            mmsl = mm(ps_y.ap()[:, 0:1], PB(29 + kt, ot * 128, 128),
                      PB(28, kt, 1), start=(kt == 0), stop=(kt == 7),
                      waits=[m_pwT[kt], m_dwbc, m_bias, mmsl])
        m_bias = act(c_pwb.ap()[:, ot:ot + 1], ps_y.ap()[:, 0:1], AF.Copy,
                     waits=[mmsl])
    m_pwb = vtt(c_pwb.ap()[:, 0:8], c_pwb.ap()[:, 0:8], colB.ap()[:, 0:8],
                ALU.add, waits=[m_pbv, m_bias])

    m_phaseC_done = None

    for b in range(BC):
        if b > 0:
            m_pwT = load_pwT([m_phaseC_done])
        P = {}
        m_stage_last = None
        for sub in range(NSUB):
            t0 = sub * TA
            mx = []
            for i in range(4):
                mx.append(dma(PF(i),
                              x_in[b, t0 + i * 128:t0 + (i + 1) * 128, :],
                              waits=[P.get(("xn", i)), m_phaseC_done,
                                     m_pwb]))
            mxn = []
            for i in range(4):
                ms1 = vred(colA.ap()[:, 0:1], PF(i),
                           waits=[mx[i], P.get(("cA",))])
                msq = act(PF(8), PF(i), AF.Square,
                          waits=[mx[i], P.get(("sq",))])
                ms2 = vred(colA.ap()[:, 1:2], PF(8), waits=[msq])
                m_mu = vts(colA.ap()[:, 2:3], colA.ap()[:, 0:1], 1.0 / D,
                           waits=[ms1])
                m_m2 = vtt(colA.ap()[:, 3:4], colA.ap()[:, 2:3],
                           colA.ap()[:, 2:3], ALU.mult, waits=[m_mu])
                m_var = vstt(colA.ap()[:, 4:5], colA.ap()[:, 1:2], 1.0 / D,
                             colA.ap()[:, 3:4], ALU.mult, ALU.subtract,
                             waits=[ms2, m_m2])
                m_ep = vts(colA.ap()[:, 4:5], colA.ap()[:, 4:5], 1.0,
                           LN_EPS, ALU.mult, ALU.add, waits=[m_var])
                m_ln = act(colA.ap()[:, 5:6], colA.ap()[:, 4:5], AF.Ln,
                           waits=[m_ep])
                m_sc = vts(colA.ap()[:, 5:6], colA.ap()[:, 5:6], -0.5,
                           waits=[m_ln])
                m_rs = act(colA.ap()[:, 6:7], colA.ap()[:, 5:6], AF.Exp,
                           waits=[m_sc])
                mxn.append(vts(PF(i), PF(i), colA.ap()[:, 2:3],
                               colA.ap()[:, 6:7], ALU.subtract, ALU.mult,
                               waits=[m_rs, msq]))
                P[("sq",)] = msq
                P[("cA",)] = mxn[-1]
            m_xq = []
            for i in range(4):
                ma = vred(colA.ap()[:, 8:9], PF(i), op=ALU.max, absv=True,
                          waits=[mxn[i], P.get(("cB",))])
                mr = vrecip(colA.ap()[:, 9:10], colA.ap()[:, 8:9],
                            waits=[ma])
                mi = vts(colA.ap()[:, 10:11], colA.ap()[:, 9:10], 127.0,
                         waits=[mr])
                msc = vts(colA.ap()[:, 11:12], colA.ap()[:, 8:9],
                          1.0 / 127.0, waits=[ma])
                mdsc = dma(AP(g_xsc.tensor, b * L + t0 + i * 128,
                              [[1, 128], [1, 1]]),
                           colA.ap()[:, 11:12], waits=[msc])
                P[("cB",)] = mdsc
                mq1 = vts(PF(8), PF(i), colA.ap()[:, 10:11],
                          waits=[mi, P[("sq",)]])
                mq2 = vts(PF(8), PF(8), RND, RND, ALU.add, ALU.subtract,
                          waits=[mq1])
                m_xq.append(vcopy(PB(i), PF(8),
                                  waits=[mq2, P.get(("xq", i))]))
                P[("sq",)] = m_xq[-1]
                P[("xq", i)] = m_xq[-1]
            m_v = []
            for i in range(4):
                msq = act(PF(8), PF(i), AF.Square, waits=[mxn[i],
                                                          P[("sq",)]])
                ms = vred(colA.ap()[:, 12:13], PF(8),
                          waits=[msq, P.get(("cC",))])
                m_ep = vts(colA.ap()[:, 12:13], colA.ap()[:, 12:13],
                           1.0 / D, RMS_EPS, ALU.mult, ALU.add, waits=[ms])
                m_ln = act(colA.ap()[:, 13:14], colA.ap()[:, 12:13], AF.Ln,
                           waits=[m_ep])
                m_sc = vts(colA.ap()[:, 13:14], colA.ap()[:, 13:14], -0.5,
                           waits=[m_ln])
                m_rs = act(colA.ap()[:, 14:15], colA.ap()[:, 13:14], AF.Exp,
                           waits=[m_sc])
                m_v.append(vts(PF(4 + i), PF(i), colA.ap()[:, 14:15],
                               waits=[m_rs, P.get(("v", i))]))
                P[("sq",)] = msq
                P[("v", i)] = m_v[-1]
                P[("cC",)] = m_v[-1]
            for dt_ in range(8):
                mts = []
                for i in range(4):
                    mts.append(tT(ps_t.ap()[:, i * 128:(i + 1) * 128],
                                  PF(4 + i, dt_ * 128, 128), idf_s.ap(),
                                  waits=[m_v[i], m_idf, P.get(("ps_t",)),
                                         mts[-1] if mts else None]))
                m_ev = act(PF(11 + dt_, 3, 512), ps_t.ap()[:, 0:512],
                           AF.Copy, waits=[mts[-1], P.get(("haloS", dt_)),
                                           P.get(("dtS", dt_))])
                P[("ps_t",)] = m_ev
                if sub == 0:
                    m_hw = vmemset(PF(11 + dt_, 0, 3), 0.0, waits=[m_ev])
                else:
                    m_hw = vcopy(PF(11 + dt_, 0, 3),
                                 halo_b.ap()[:, dt_ * 3:dt_ * 3 + 3],
                                 waits=[m_ev, P.get(("haloS", dt_))])
                mc = vts(PF(19 + dt_, 0, 512), PF(11 + dt_, 0, 512),
                         kcol.ap()[:, dt_ * 4:dt_ * 4 + 1],
                         waits=[m_ev, m_hw, m_kf, P.get(("xh", dt_))])
                for k in range(1, 4):
                    if k == 3:
                        dst = PB(4 + dt_ // 2, (dt_ % 2) * 512, 512)
                        w_x = P.get(("ucb", dt_))
                    else:
                        dst = PF(19 + dt_, 0, 512)
                        w_x = None
                    mc = vstt(dst, PF(11 + dt_, k, 512),
                              kcol.ap()[:, dt_ * 4 + k:dt_ * 4 + k + 1],
                              PF(19 + dt_, 0, 512), ALU.mult, ALU.add,
                              waits=[mc, w_x])
                P[("ucb", dt_)] = mc
                P[("conv", dt_)] = mc
                m_hs = vcopy(halo_b.ap()[:, dt_ * 3:dt_ * 3 + 3],
                             PF(11 + dt_, TA, 3), waits=[mc])
                P[("haloS", dt_)] = m_hs
                mts2 = []
                for i in range(4):
                    mts2.append(tT(ps_t.ap()[:, i * 128:(i + 1) * 128],
                                   PF(i, dt_ * 128, 128), idf_s.ap(),
                                   waits=[mxn[i], P[("ps_t",)],
                                          mts2[-1] if mts2 else None]))
                m_ev2 = act(PF(8, 0, 512), ps_t.ap()[:, 0:512], AF.Copy,
                            waits=[mts2[-1], P[("sq",)]])
                P[("ps_t",)] = m_ev2
                m_st = dma(AP(g_xn.tensor, (b * D + dt_ * 128) * L + t0,
                              [[L, 128], [1, 512]]), PF(8, 0, 512),
                           waits=[m_ev2])
                P[("sq",)] = m_st
                P[("xn", 0)] = P[("xn", 1)] = P[("xn", 2)] = P[("xn", 3)] \
                    = m_st if dt_ == 7 else P.get(("xn", 0))
                mts3 = []
                for i in range(4):
                    mts3.append(tT(ps_tb.ap()[:, i * 128:(i + 1) * 128],
                                   PB(i, dt_ * 128, 128), idb_s.ap(),
                                   waits=[m_xq[i], ps_tb_last[0],
                                          mts3[-1] if mts3 else None]))
                m_ev3 = act(PB(38 + dt_ % 2, 0, 512), ps_tb.ap()[:, 0:512],
                            AF.Copy, waits=[mts3[-1],
                                            P.get(("xqS", dt_ % 2))])
                ps_tb_last[0] = m_ev3
                m_sq = dma(AP(g_xq.tensor, (b * D + dt_ * 128) * L + t0,
                              [[L, 128], [1, 512]]),
                           PB(38 + dt_ % 2, 0, 512), waits=[m_ev3])
                P[("xqS", dt_ % 2)] = m_sq
                if dt_ == 7:
                    for i in range(4):
                        P[("xq", i)] = m_sq
            for o8 in range(8):
                b_, bank, bfree = BK.get()
                mmsl = None
                for kt in range(8):
                    mmsl = mm(bank, PB(29 + kt, o8 * 128, 128),
                              PB(4 + kt // 2, (kt % 2) * 512, 512),
                              start=(kt == 0), stop=(kt == 7),
                              waits=[m_pwT[kt], P[("conv", kt)], bfree,
                                     mmsl])
                m_xh = act(PF(19 + o8, 0, 512), bank, AF.Identity,
                           bias=c_pwb.ap()[:, o8:o8 + 1],
                           waits=[mmsl, m_pwb, P[("conv", o8)]])
                BK.done(b_, m_xh)
                P[("xh", o8)] = m_xh
                m_xhb = vcopy(PB(8 + o8 // 2, (o8 % 2) * 512, 512),
                              PF(19 + o8, 0, 512),
                              waits=[m_xh, P.get(("xhb", o8))])
                P[("xhb", o8)] = m_xhb
            m_dtp = {}
            for h_ in range(16):
                hp, hl = h_ // 2, h_ % 2
                rhs = PB(8 + hp // 2, (hp % 2) * 512, 512, r0=64 * hl,
                         rows=64)
                w_rhs = P[("xhb", hp)]
                b1, bank1, bf1 = BK.get()
                mdtx = mm(AP(ps_mm, bank1.offset + (64 * hl) * 2048,
                             [[2048, 64], [1, 512]]),
                          AP(xwT_s, (64 * hl) * 160, [[160, 64], [1, 64]]),
                          rhs, True, True, waits=[w_rhs, m_xwC, bf1])
                m_dtxe = act(AP(s_dtx, (64 * hl) * (8 * TA) + hp * TA,
                                [[8 * TA, 64], [1, TA]]),
                             AP(ps_mm, bank1.offset + (64 * hl) * 2048,
                                [[2048, 64], [1, 512]]),
                             AF.Copy,
                             waits=[mdtx, P.get(("dtx", h_))])
                BK.done(b1, m_dtxe)
                P[("dtx", h_)] = m_dtxe
                b2, bank2, bf2 = BK.get()
                mB = mm(AP(ps_mm, bank2.offset + (64 * hl) * 2048,
                           [[2048, 48], [1, 512]]),
                        AP(xwT_s, (64 * hl) * 160 + 64, [[160, 64], [1, 48]]),
                        rhs, True, True, waits=[w_rhs, m_xwC, bf2])
                m_Be = act(AP(s_bc, (0) * TA if hl == 0 else 0, [[2 * TA, 48], [1, TA]])
                           if hl == 0 else AP(s_bc, 0, [[2 * TA, 48], [1, TA]]),
                           AP(ps_mm, bank2.offset + (64 * hl) * 2048,
                              [[2048, 48], [1, 512]]),
                           AF.Copy, waits=[mB, P.get(("bcS", 0))])
                BK.done(b2, m_Be)
                m_Bs = dma(AP(g_bt.tensor, (b * NL + h_) * L + t0,
                              [[16 * L, 48], [1, 512]]),
                           s_bc.ap()[:, 0:TA], waits=[m_Be])
                P[("bcS", 0)] = m_Bs
                b3, bank3, bf3 = BK.get()
                mC = mm(AP(ps_mm, bank3.offset + (64 * hl) * 2048,
                           [[2048, 48], [1, 512]]),
                        AP(xwT_s, (64 * hl) * 160 + 112, [[160, 64], [1, 48]]),
                        rhs, True, True, waits=[w_rhs, m_xwC, bf3])
                m_Ce = act(AP(s_bc, TA, [[2 * TA, 48], [1, TA]]),
                           AP(ps_mm, bank3.offset + (64 * hl) * 2048,
                              [[2048, 48], [1, 512]]),
                           AF.Copy, waits=[mC, P.get(("bcS2", 0))])
                BK.done(b3, m_Ce)
                m_Cs = dma(AP(g_ct.tensor, (b * NL + h_) * L + t0,
                              [[16 * L, 48], [1, 512]]),
                           s_bc.ap()[:, TA:2 * TA], waits=[m_Ce])
                P[("bcS2", 0)] = m_Cs
                m_stage_last = m_Cs
                b4, bank4, bf4 = BK.get()
                mdt = mm(AP(ps_mm, bank4.offset + (64 * hl) * 2048,
                            [[2048, 64], [1, 512]]),
                         AP(dwT_s, (64 * hl) * 64, [[64, 64], [1, 64]]),
                         AP(s_dtx, (64 * hl) * (8 * TA) + hp * TA,
                            [[8 * TA, 64], [1, TA]]), True, True,
                         waits=[m_dtxe, m_dwT, bf4])
                m_dte = act(PF(11 + hp, 0, 512, r0=64 * hl, rows=64),
                            AP(ps_mm, bank4.offset + (64 * hl) * 2048,
                               [[2048, 64], [1, 512]]),
                            AF.Identity,
                            bias=c_dtb.ap()[0:64, 0:1],
                            waits=[mdt, m_dtb, P[("conv", hp)],
                                   P[("haloS", hp)], P.get(("dtS", hp))])
                BK.done(b4, m_dte)
                m_dtp[h_] = m_dte
            for j in range(8):
                wj = [m_dtp[2 * j], m_dtp[2 * j + 1]]
                mab = act(PF(8, 0, 512), PF(11 + j, 0, 512), AF.Abs,
                          waits=wj + [P[("sq",)]])
                mex = act(PF(8, 0, 512), PF(8, 0, 512), AF.Exp, scale=-1.0,
                          waits=[mab])
                mln = act(PF(8, 0, 512), PF(8, 0, 512), AF.Ln, bias=1.0,
                          waits=[mex])
                mre = act(PF(9, 0, 512), PF(11 + j, 0, 512), AF.Relu,
                          waits=wj + [P.get(("sp9",))])
                mdt2 = vtt(PF(11 + j, 0, 512), PF(9, 0, 512), PF(8, 0, 512),
                           ALU.add, waits=[mln, mre] + wj)
                P[("sq",)] = mdt2
                P[("sp9",)] = mdt2
                mhi = vcopy(PB(12 + j // 2, (j % 2) * 512, 512),
                            PF(11 + j, 0, 512),
                            waits=[mdt2, P.get(("dthi", j))])
                mlo1 = vtt(PF(9, 0, 512), PF(11 + j, 0, 512),
                           PB(12 + j // 2, (j % 2) * 512, 512),
                           ALU.subtract, waits=[mhi])
                mlo = vcopy(PB(16 + j // 2, (j % 2) * 512, 512),
                            PF(9, 0, 512),
                            waits=[mlo1, P.get(("dtlo", j))])
                mdu = vtt(PB(20 + j // 2, (j % 2) * 512, 512),
                          PF(11 + j, 0, 512), PF(19 + j, 0, 512), ALU.mult,
                          waits=[mdt2, P[("xh", j)], P.get(("dtub", j))])
                P[("dthi", j)] = mhi
                P[("dtlo", j)] = mlo
                P[("dtub", j)] = mdu
                P[("sp9",)] = mlo
                P[("xh", j)] = mdu
                for hl in range(2):
                    h_ = 2 * j + hl
                    for x_ in range(2):
                        tile_i = (12 if x_ == 0 else 16) + j // 2
                        src_ap = AP(pb, (64 * hl) * (PB_N * 1024)
                                    + tile_i * 1024 + (j % 2) * 512,
                                    [[PB_N * 1024, 64], [1, TC]])
                        m_s = dma(
                            AP(g_dt.tensor,
                               (b * NDHG * NTC + sub) * DT_BLK
                               + (h_ * 2 + x_) * TC,
                               [[NTC * DT_BLK, NDHG], [32 * TC, DH_SUB],
                                [1, TC]]),
                            src_ap, waits=[mhi if x_ == 0 else mlo])
                        m_stage_last = m_s
                    src_ap = AP(pb, (64 * hl) * (PB_N * 1024)
                                + (20 + j // 2) * 1024 + (j % 2) * 512,
                                [[PB_N * 1024, 64], [1, TC]])
                    m_s = dma(
                        AP(g_du.tensor,
                           (b * NDHG * NTC + sub) * DU_BLK + h_ * TC,
                           [[NTC * DU_BLK, NDHG], [16 * TC, DH_SUB],
                            [1, TC]]),
                        src_ap, waits=[mdu])
                    m_stage_last = m_s
                P[("dtS", j)] = m_stage_last

        m_bt = dma(AP(pb, 0, [[PB_N * 1024, 128], [1, NT * L]]),
                   AP(g_bt.tensor, b * NL * L,
                      [[L, 128], [128 * L, NT], [1, L]]),
                   waits=[m_stage_last, P[("xq", 0)], P[("bcS", 0)],
                          P[("bcS2", 0)], P[("xqS", 0)], P[("xqS", 1)]])
        m_ct = dma(AP(pb, 12 * 1024, [[PB_N * 1024, 128], [1, NT * L]]),
                   AP(g_ct.tensor, b * NL * L,
                      [[L, 128], [128 * L, NT], [1, L]]),
                   waits=[m_stage_last])
        chunks = [(dhg, tc_) for dhg in range(NDHG) for tc_ in range(NTC)]
        NCH = len(chunks)
        m_repmm = [None] * NCH
        m_dumm = [None] * NCH
        m_duev = [None] * NCH
        m_exps = [None] * NCH
        m_rdt = [None] * NCH
        m_rdu = [None] * NCH
        m_abuf = {}
        m_abuf0 = [None]

        def a_tile(Pp_, ti_):
            if ti_ == 0:
                return 0
            return 2 * ti_ if Pp_ == 0 else 10 + 2 * ti_
        m_wfree = [None, None]
        m_wlast = [None, None]
        m_mfree = [None, None]
        scan_st = {"h": None, "yevl": None}
        m_carry = [None] * NT
        m_ydma = [None] * 4
        YQ = [(ps_y, 0), (ps_y, 32), (ps_r, 0), (ps_r, 32)]

        def chunk_prep(k):
            dhg, tc_ = chunks[k]
            Pp = k % 2
            blk = (b * NDHG + dhg) * NTC + tc_
            w_pre = [m_bt, m_ct] if k == 0 else []
            mrdt_n = []
            for p_ in range(2):
                mrdt_n.append(dma(
                    AP(s_rd, p_ * TC, [[2 * TC, 64], [1, TC]]),
                    AP(g_dt.tensor, blk * DT_BLK + p_ * 64 * TC,
                       [[TC, 64], [1, TC]]),
                    waits=[m_repmm[k - 1] if k else None,
                           m_rdt[k - 1] if k else None] + w_pre))
            m_rdt[k] = mrdt_n[-1]
            mrdu_n = []
            for p_ in range(2):
                mrdu_n.append(dma(
                    AP(s_rd, 64 * 2 * TC + p_ * TC, [[2 * TC, 32], [1, TC]]),
                    AP(g_du.tensor, blk * DU_BLK + p_ * 32 * TC,
                       [[TC, 32], [1, TC]]),
                    waits=[m_dumm[k - 1] if k else None,
                           m_rdu[k - 1] if k else None]))
            m_rdu[k] = mrdu_n[-1]
            mmx = m_duev[k - 1] if k else None
            for d in range(DH_SUB):
                p_ = d // 2
                mmx = mm(ps_mm.ap()[:, d * TC:(d + 1) * TC],
                         AP(s_pd, d * 128, [[DH_SUB * 128, 64], [1, 128]]),
                         AP(s_rd, p_ * TC, [[2 * TC, 64], [1, TC]]),
                         True, True, waits=[mrdt_n[p_], m_pdt, mmx])
            m_repmm[k] = mmx
            m_exp = []
            for ti in range(NT):
                m_exp.append(act(PF(a_tile(Pp, ti), 0, 2048), ps_mm.ap(),
                                 AF.Exp, scale=s_acol.ap()[:, ti:ti + 1],
                                 waits=[m_repmm[k], m_acol,
                                        m_abuf0[0] if ti == 0
                                        else m_abuf.get((Pp, ti))]))
            m_exps[k] = m_exp
            mmx = m_exp[-1]
            for d in range(DH_SUB):
                p_ = d // 2
                mmx = mm(ps_mm.ap()[:, d * TC:(d + 1) * TC],
                         AP(s_pd, 64 * DH_SUB * 128 + d * 128,
                            [[DH_SUB * 128, 32], [1, 128]]),
                         AP(s_rd, 64 * 2 * TC + p_ * TC,
                            [[2 * TC, 32], [1, TC]]),
                         True, True, waits=[mrdu_n[p_], m_pdu, mmx])
            m_dumm[k] = mmx
            m_duev[k] = act(PB(24 + 2 * Pp, 0, 2048), ps_mm.ap(), AF.Copy,
                            waits=[m_dumm[k], m_wlast[Pp]])

        h_free = [None, None]

        def chunk_tiles(k, prep_next=None):
            dhg, tc_ = chunks[k]
            Pp = k % 2
            du_i = 24 + 2 * Pp
            m_ws = [None] * NT

            def emit_w(ti):
                pp = ti % 2
                m_ws[ti] = gtt(
                    PB(28 + pp * 2, 0, 2048).rearrange(
                        "p (d t) -> p d t", d=DH_SUB),
                    PB(du_i, 0, 2048).rearrange("p (d t) -> p d t",
                                                d=DH_SUB),
                    AP(pb, ti * L + tc_ * TC,
                       [[PB_N * 1024, 128], [0, DH_SUB], [1, TC]]),
                    ALU.mult,
                    waits=[m_duev[k], m_bt, m_wfree[pp]])
                m_wlast[Pp] = m_ws[ti]

            emit_w(0)
            emit_w(1)
            for ti in range(NT):
                if ti == 1 and prep_next is not None:
                    prep_next()
                pp = ti % 2
                a_i = a_tile(Pp, ti)
                w_i = 28 + pp * 2
                m_i = 32 + pp * 2
                h_i = 22 + 2 * (ti % 2)
                a_seg = AP(pf, a_i * 1024, [[PF_N * 1024, 128],
                                            [TC, DH_SUB]])
                w_seg = AP(pb, w_i * 1024, [[PB_N * 1024, 128],
                                            [TC, DH_SUB]])
                m_w = m_ws[ti]
                cs = s_carry.ap()[:, ti * DH_SUB:(ti + 1) * DH_SUB]
                if tc_ == 0:
                    m_fx = vmemset(a_seg, 0.0, waits=[m_exps[k][ti]])
                else:
                    m_t1 = vtt(s_tmp.ap(), a_seg, cs, ALU.mult,
                               waits=[m_exps[k][ti], m_carry[ti]])
                    m_t2 = m_t1
                    for d in range(DH_SUB):
                        wcol = AP(pb, w_i * 1024 + d * TC,
                                  [[PB_N * 1024, 128], [1, 1]])
                        m_t2 = vtt(wcol, wcol,
                                   s_tmp.ap()[:, d:d + 1], ALU.add,
                                   waits=[m_t2, m_w if d == 0 else None])
                    m_fx = vmemset(a_seg, 0.0, waits=[m_t2])
                m_scan = vscan(PF(h_i, 0, 2048), PF(a_i, 0, 2048),
                               PB(w_i, 0, 2048),
                               waits=[m_fx, m_w, h_free[ti % 2]])
                if ti == 0:
                    m_abuf0[0] = m_scan
                else:
                    m_abuf[(Pp, ti)] = m_scan
                m_wfree[pp] = m_scan
                if ti + 2 < NT:
                    emit_w(ti + 2)
                m_cw = dma(cs, AP(pf, h_i * 1024 + TC - 1,
                                  [[PF_N * 1024, 128], [TC, DH_SUB]]),
                           waits=[m_scan], slow=True)
                m_carry[ti] = m_cw
                ct_sl = AP(pb, 12 * 1024 + ti * L + tc_ * TC,
                           [[PB_N * 1024, 128], [1, TC]])
                mqs = []
                for d in range(DH_SUB):
                    on_v = d == 0 or (d == 1 and ti % 2 == 0)
                    f_tt = vtt if on_v else gtt
                    mqs.append(f_tt(PB(m_i, d * TC, TC),
                                    PF(h_i, d * TC, TC), ct_sl, ALU.mult,
                                    waits=[m_scan, m_ct, m_mfree[pp]]))
                h_free[ti % 2] = mqs + [m_cw]
                for qt in range(4):
                    pq, r0 = YQ[qt]
                    m_yq = mm(AP(pq, r0 * 512, [[512, 16], [1, 512]]),
                              s_yd.ap(), PB(m_i, qt * 512, 512),
                              start=(ti == 0), stop=(ti == NT - 1),
                              waits=[mqs[qt], m_yd,
                                     scan_st["yevl"] if ti == 0 else None],
                              skip=True)
                    m_mfree[pp] = m_yq
                if ti == NT - 1:
                    m_evs = []
                    for qt in range(4):
                        pq, r0 = YQ[qt]
                        m_evs.append(act(
                            AP(s_yev, qt * 512, [[2048, 16], [1, 512]]),
                            AP(pq, r0 * 512, [[512, 16], [1, 512]]),
                            AF.Copy,
                            waits=[m_mfree[pp], m_ydma[qt]]))
                    scan_st["yevl"] = m_evs[-1]
                    for qt in range(4):
                        m_ydma[qt] = dma(
                            AP(g_y.tensor,
                               (b * D + dhg * DH_SUB + qt) * L + tc_ * TC,
                               [[64 * L, 16], [1, TC]]),
                            AP(s_yev, qt * 512, [[2048, 16], [1, 512]]),
                            waits=[m_evs[qt]])

        chunk_prep(0)
        for k in range(NCH):
            chunk_tiles(k, prep_next=(lambda kk=k + 1: chunk_prep(kk))
                        if k + 1 < NCH else None)
        m_scan_done = [m_ydma[qt] for qt in range(4)]
        m_hbuf = [x for x in h_free[0] + h_free[1] if x is not None]

        m_opTs = []
        for kt in range(8):
            m_opTs.append(dma(PB(kt, 0, 1024),
                              AP(g_opT.tensor, kt * 128 * D,
                                 [[D, 128], [1, 1024]]),
                              waits=[m_hbuf, m_wlast[0], m_wlast[1],
                                     m_mfree[0], m_mfree[1], m_duev[-1],
                                     scan_st["yevl"]] + m_scan_done))
        PC = {}
        m_h2S = {}
        m_x1S = {}
        for sub in range(NSUB):
            t0 = sub * TA
            m_yb = []
            for j in range(8):
                m_yb.append(dma(PB(32 + j // 2, (j % 2) * 512, 512),
                                AP(g_y.tensor, (b * D + j * 128) * L + t0,
                                   [[L, 128], [1, 512]]),
                                waits=[PC.get(("yb", j)),
                                       PC.get(("ybs", j))]
                                + ([m_mfree[0], m_mfree[1], m_hbuf]
                                   + m_scan_done if sub == 0 else [])))
            m_mm_last = []
            for o8 in range(8):
                b_, bank, bfree = BK.get()
                mmsl = None
                for kt in range(8):
                    mmsl = mm(bank, PB(kt, o8 * 128, 128),
                              PB(32 + kt // 2, (kt % 2) * 512, 512),
                              start=(kt == 0), stop=(kt == 7),
                              waits=[m_opTs[kt], m_yb[kt], bfree, mmsl])
                m_mm_last.append(mmsl)
                m_oe = act(PF(8 + o8, 0, 512), bank, AF.Identity,
                           bias=c_ob.ap()[:, o8:o8 + 1],
                           waits=[mmsl, m_ob, PC.get(("sq", o8))])
                BK.done(b_, m_oe)
                m_xl = dma(PF(16 + o8, 0, 512),
                           AP(g_xn.tensor, (b * D + o8 * 128) * L + t0,
                              [[L, 128], [1, 512]]),
                           waits=[PC.get(("xn", o8))]
                           + ([m_hbuf] if sub == 0 else []))
                m_x1 = vtt(PF(o8, 0, 512), PF(8 + o8, 0, 512),
                           PF(16 + o8, 0, 512), ALU.add,
                           waits=[m_oe, m_xl, PC.get(("x1s", o8)),
                                  PC.get(("x1", o8))])
                PC[("x1", o8)] = m_x1
                PC[("xn", o8)] = m_x1
                m_x1s = dma(AP(g_x1.tensor, (b * D + o8 * 128) * L + t0,
                               [[L, 128], [1, 512]]),
                            PF(o8, 0, 512), waits=[m_x1])
                m_x1S[(sub, o8)] = m_x1s
                PC[("x1s", o8)] = m_x1s
            for o8 in range(8):
                m_sq = act(PF(8 + o8, 0, 512), PF(o8, 0, 512), AF.Square,
                           waits=[PC[("x1", o8)], PC.get(("sq", o8))])
                PC[("sq", o8)] = m_sq
                m_hi = vcopy(PB(36 + o8 // 2, (o8 % 2) * 512, 512),
                             PF(8 + o8, 0, 512),
                             waits=[m_sq, PC.get(("sqh", o8))])
                m_lo1 = vtt(PF(8 + o8, 0, 512), PF(8 + o8, 0, 512),
                            PB(36 + o8 // 2, (o8 % 2) * 512, 512),
                            ALU.subtract, waits=[m_hi])
                m_lo = vcopy(PB(32 + o8 // 2, (o8 % 2) * 512, 512),
                             PF(8 + o8, 0, 512),
                             waits=[m_lo1, m_mm_last[-1],
                                    PC.get(("sql", o8))])
                PC[("sqh", o8)] = m_hi
                PC[("sql", o8)] = m_lo
                PC[("sq", o8)] = m_lo
            mmsl = PC.get(("msrow",))
            first = True
            for j in range(8):
                for which in range(2):
                    t_i = (36 if which == 0 else 32) + j // 2
                    mmsl = mm(ps_y.ap(), ones_c.ap(),
                              PB(t_i, (j % 2) * 512, 512),
                              start=first, stop=(j == 7 and which == 1),
                              waits=[PC[("sqh", j)] if which == 0
                                     else PC[("sql", j)],
                                     m_on1, mmsl])
                    first = False
            m_pre = vts(PF(24, 0, 512), ps_y.ap(), 1.0 / D, RMS_EPS,
                        ALU.mult, ALU.add,
                        waits=[mmsl, PC.get(("msrow2",)), PC.get(("rowb",))])
            m_ms = act(PF(24, 0, 512), PF(24, 0, 512), AF.Ln,
                       waits=[m_pre])
            PC[("msrow",)] = m_ms
            m_sc2 = vts(PF(24, 0, 512), PF(24, 0, 512), -0.5, waits=[m_ms])
            m_rep = act(PF(24, 0, 512), PF(24, 0, 512), AF.Exp,
                       waits=[m_sc2])
            PC[("rowb",)] = m_rep
            PC[("msrow2",)] = m_rep
            for o8 in range(8):
                m_h2 = vstt(PB(32 + o8 // 2, (o8 % 2) * 512, 512),
                            PF(o8, 0, 512), c_n2.ap()[:, o8:o8 + 1],
                            PF(24, 0, 512), ALU.mult, ALU.mult,
                            waits=[PC[("x1", o8)], m_rep, m_n2])
                PC[("yb", o8)] = m_h2
                PC[("h2rep",)] = m_h2
                m_h2s = dma(AP(g_h2.tensor, (b * D + o8 * 128) * L + t0,
                               [[L, 128], [1, 512]]),
                            PB(32 + o8 // 2, (o8 % 2) * 512, 512),
                            waits=[m_h2])
                m_h2S[(sub, o8)] = m_h2s
                PC[("ybs", o8)] = m_h2s

        m_gws, m_uws = [], []
        for kt in range(8):
            m_gws.append(dma(
                AP(pb, (2 * kt) * 1024, [[PB_N * 1024, 128], [1, 2048]]),
                AP(g_gwT.tensor, kt * 128 * DI, [[DI, 128], [1, 2048]]),
                waits=[m_h2S[(NSUB - 1, 7)], m_opTs[kt]]))
        for kt in range(8):
            m_uws.append(dma(
                AP(pb, (16 + 2 * kt) * 1024, [[PB_N * 1024, 128], [1, 2048]]),
                AP(g_uwT.tensor, kt * 128 * DI, [[DI, 128], [1, 2048]]),
                waits=[m_h2S[(NSUB - 1, 7)]]))
        PC2 = {}
        m_midS = {}
        for sub in range(NSUB):
            t0 = sub * TA
            m_h2l = []
            for j in range(8):
                m_h2l.append(dma(PB(32 + j // 2, (j % 2) * 512, 512),
                                 AP(g_h2.tensor, (b * D + j * 128) * L + t0,
                                    [[L, 128], [1, 512]]),
                                 waits=[PC2.get(("h2", j)),
                                        m_h2S[(sub, j)]] +
                                 ([m_uws[-1]] if sub == 0 else [])))
                PC2[("h2", j)] = m_h2l[-1]
            for o16 in range(16):
                bg, bankg, bfg = BK.get()
                mmsl = None
                for kt in range(8):
                    mmsl = mm(bankg,
                              AP(pb, (2 * kt) * 1024 + o16 * 128,
                                 [[PB_N * 1024, 128], [1, 128]]),
                              PB(32 + kt // 2, (kt % 2) * 512, 512),
                              start=(kt == 0), stop=(kt == 7),
                              waits=[m_gws[kt], m_h2l[kt], bfg, mmsl])
                m_sg = act(PB(36 + o16 % 2, 0, 512), bankg, AF.Silu,
                           waits=[mmsl, PC2.get(("sgb", o16 % 2))])
                BK.done(bg, m_sg)
                bu, banku, bfu = BK.get()
                mmsl = None
                for kt in range(8):
                    mmsl = mm(banku,
                              AP(pb, (16 + 2 * kt) * 1024 + o16 * 128,
                                 [[PB_N * 1024, 128], [1, 128]]),
                              PB(32 + kt // 2, (kt % 2) * 512, 512),
                              start=(kt == 0), stop=(kt == 7),
                              waits=[m_uws[kt], m_h2l[kt], bfu, mmsl])
                m_ue = act(PB(38 + o16 % 2, 0, 512), banku, AF.Copy,
                           waits=[mmsl, PC2.get(("ueb", o16 % 2))])
                BK.done(bu, m_ue)
                m_mid = vtt(PB(36 + o16 % 2, 0, 512),
                            PB(36 + o16 % 2, 0, 512),
                            PB(38 + o16 % 2, 0, 512), ALU.mult,
                            waits=[m_sg, m_ue])
                PC2[("ueb", o16 % 2)] = m_mid
                m_ms = dma(AP(g_mid.tensor, (b * DI + o16 * 128) * L + t0,
                              [[L, 128], [1, 512]]),
                           PB(36 + o16 % 2, 0, 512), waits=[m_mid])
                PC2[("sgb", o16 % 2)] = m_ms
                m_midS[(sub, o16)] = m_ms

        m_dns, m_fqs = [], []
        for kt in range(16):
            m_dns.append(dma(PB(kt, 0, 1024),
                             AP(g_dnT.tensor, kt * 128 * D,
                                [[D, 128], [1, 1024]]),
                             waits=[m_midS[(NSUB - 1, 15)],
                                    (m_gws + m_uws)[kt]]))
        for kt in range(8):
            m_fqs.append(dma(PB(16 + kt, 0, 1024),
                             AP(g_fqT.tensor, kt * 128 * D,
                                [[D, 128], [1, 1024]]),
                             waits=[m_midS[(NSUB - 1, 15)],
                                    m_uws[kt]]))
        PC3 = {}
        m_out_last = None
        for sub in range(NSUB):
            t0 = sub * TA
            m_midl = []
            for j in range(16):
                m_midl.append(dma(PB(24 + j // 2, (j % 2) * 512, 512),
                                  AP(g_mid.tensor,
                                     (b * DI + j * 128) * L + t0,
                                     [[L, 128], [1, 512]]),
                                  waits=[PC3.get(("mid", j)),
                                         m_midS[(sub, j)]] +
                                  ([m_dns[-1], m_fqs[-1]] if sub == 0
                                   else [])))
                PC3[("mid", j)] = m_midl[-1]
            m_xql = []
            for j in range(8):
                m_xql.append(dma(PB(32 + j // 2, (j % 2) * 512, 512),
                                 AP(g_xq.tensor, (b * D + j * 128) * L + t0,
                                    [[L, 128], [1, 512]]),
                                 waits=[PC3.get(("xq", j))] +
                                 ([m_fqs[-1]] if sub == 0 else [])))
                PC3[("xq", j)] = m_xql[-1]
            if sub == 0:
                m_z36 = vmemset(PB(36, 0, 512), 0.0,
                                waits=[PC2.get(("sgb", 0)),
                                       PC2.get(("ueb", 0))])
                PC3[("z36",)] = m_z36
            m_xsl = dma(PF(26, 0, 512, rows=1),
                        AP(g_xsc.tensor, b * L + t0, [[TA, 1], [1, TA]]),
                        waits=[PC3.get(("xsc",)), PC.get(("rowb",))])
            m_xsb = vcopy(PB(36, 0, 512, rows=1), PF(26, 0, 512, rows=1),
                          waits=[m_xsl, PC3.get(("xscb",)),
                                 PC3.get(("z36",))])
            m_xrep = mm(ps_r.ap(), e0_s.ap(), PB(36, 0, 512),
                        True, True,
                        waits=[m_xsb, m_on2, PC3.get(("t2rep",))])
            PC3[("xsc",)] = m_xsb
            PC3[("xscb",)] = m_xrep
            for o8 in range(8):
                bd, bankd, bfd = BK.get()
                mmsl = None
                for kt in range(16):
                    mmsl = mm(bankd, PB(kt, o8 * 128, 128),
                              PB(24 + kt // 2, (kt % 2) * 512, 512),
                              start=(kt == 0), stop=(kt == 15),
                              waits=[m_dns[kt], m_midl[kt], bfd, mmsl])
                m_de = act(PF(8 + o8, 0, 512), bankd, AF.Copy,
                           waits=[mmsl, PC3.get(("pf8", o8))])
                BK.done(bd, m_de)
                m_x1l = dma(PF(16 + o8, 0, 512),
                            AP(g_x1.tensor, (b * D + o8 * 128) * L + t0,
                               [[L, 128], [1, 512]]),
                            waits=[PC3.get(("x1", o8)), m_x1S[(sub, o8)]])
                m_s = vtt(PF(o8, 0, 512), PF(8 + o8, 0, 512),
                          PF(16 + o8, 0, 512), ALU.add,
                          waits=[m_de, m_x1l, PC3.get(("s", o8))])
                PC3[("s", o8)] = m_s
                bq, bankq, bfq = BK.get()
                mmsl = None
                for kt in range(8):
                    mmsl = mm(bankq, PB(16 + kt, o8 * 128, 128),
                              PB(32 + kt // 2, (kt % 2) * 512, 512),
                              start=(kt == 0), stop=(kt == 7),
                              waits=[m_fqs[kt], m_xql[kt], bfq, mmsl])
                m_t1 = act(PF(16 + o8, 0, 512), bankq, AF.Identity,
                           scale=c_ws.ap()[:, o8:o8 + 1],
                           waits=[mmsl, m_s, m_x1l])
                BK.done(bq, m_t1)
                PC3[("x1", o8)] = m_t1
                m_t2 = vtt(PF(16 + o8, 0, 512), PF(16 + o8, 0, 512),
                           ps_r.ap(), ALU.mult,
                           waits=[m_t1, m_xrep])
                PC3[("t2rep",)] = m_t2
                m_g = act(PF(8 + o8, 0, 512), PF(16 + o8, 0, 512), AF.Silu,
                          bias=c_fb.ap()[:, o8:o8 + 1],
                          waits=[m_t2, m_fb, m_de])
                m_dbg = None
                if DBG:
                    m_d1 = dma(AP(g_dbg_s.tensor,
                                  (b * D + o8 * 128) * L + t0,
                                  [[L, 128], [1, 512]]),
                               PF(o8, 0, 512), waits=[m_s])
                    m_d2 = dma(AP(g_dbg_g.tensor,
                                  (b * D + o8 * 128) * L + t0,
                                  [[L, 128], [1, 512]]),
                               PF(8 + o8, 0, 512), waits=[m_g])
                    m_dbg = [m_d1, m_d2]
                m_sg2 = vtt(PF(o8, 0, 512), PF(o8, 0, 512),
                            PF(8 + o8, 0, 512), ALU.mult,
                            waits=[m_g, m_s, m_dbg])
                PC3[("pf8", o8)] = m_sg2
                PC3[("s", o8)] = m_sg2
            for i in range(4):
                mts = []
                for half in range(2):
                    for j in range(4):
                        o8 = half * 4 + j
                        mts.append(tT(ps_t.ap()[:, j * 128:(j + 1) * 128],
                                      PF(o8, i * 128, 128), idf_s.ap(),
                                      waits=[PC3[("s", o8)],
                                             PC3.get(("ps_t",)),
                                             mts[-1] if mts else None]))
                    m_ev = act(PF(24, half * 512, 512),
                               ps_t.ap()[:, 0:512], AF.Copy,
                               waits=[mts[-1], PC3.get(("pf24",))])
                    PC3[("ps_t",)] = m_ev
                m_xld = dma(PF(25), x_in[b, t0 + i * 128:t0 + (i + 1) * 128,
                                        :],
                            waits=[PC3.get(("pf25",))])
                m_o = vtt(PF(25), PF(25), PF(24), ALU.add,
                          waits=[m_ev, m_xld])
                PC3[("out", i)] = m_o
                PC3[("pf24",)] = m_o
                m_out_last = dma(
                    AP(out_t.tensor, (b * L + t0 + i * 128) * D,
                       [[D, 128], [1, D]]),
                    PF(25), waits=[m_o])
                PC3[("pf25",)] = m_out_last
        m_phaseC_done = m_out_last

    with (
        nc.semaphore("sem_q") as sem_q,
        nc.semaphore("sem_t") as sem_t,
        nc.semaphore("sem_a") as sem_a,
        nc.semaphore("sem_v") as sem_v,
        nc.semaphore("sem_g") as sem_g,
        nc.Block() as block,
    ):
        S.render(block, {"q": sem_q, "t": sem_t, "a": sem_a,
                         "v": sem_v, "g": sem_g})
    return nc


def kernel(**inputs):
    x = np.ascontiguousarray(inputs["L0_attn_x"], dtype=np.float32)
    pat_dt, pat_du, yd = host_tables()
    lane_h_full = np.array([l % 16 for l in range(NL)])
    lane_s_full = np.array([l // 16 for l in range(NL)])
    alog = np.asarray(inputs["L0_attn_A_log"], dtype=np.float32)
    alog_lane = alog[lane_h_full, lane_s_full]
    t_alog = alog_lane.reshape(NT, 128).T.copy()
    idf = np.eye(128, dtype=np.float32)
    idb = np.eye(128, dtype=np.float32).astype(bf)

    common = dict(
        norm1_w=np.asarray(inputs["L0_attn_norm1_w"], np.float32),
        norm2_w=np.asarray(inputs["L0_attn_norm2_w"], np.float32),
        dw_kernel=np.asarray(inputs["L0_attn_dw_kernel"], np.float32),
        dw_bias=np.asarray(inputs["L0_attn_dw_bias"], np.float32),
        pw_w=np.asarray(inputs["L0_attn_pw_w"], np.float32),
        pw_b=np.asarray(inputs["L0_attn_pw_b"], np.float32),
        xp_w=np.asarray(inputs["L0_attn_x_proj_w"], np.float32),
        dtp_w=np.asarray(inputs["L0_attn_dt_proj_w"], np.float32),
        dtp_b=np.asarray(inputs["L0_attn_dt_proj_b"], np.float32),
        out_w=np.asarray(inputs["L0_attn_out_proj_w"], np.float32),
        out_b=np.asarray(inputs["L0_attn_out_proj_b"], np.float32),
        gate_w=np.asarray(inputs["L0_attn_gate_w"], np.float32),
        up_w=np.asarray(inputs["L0_attn_up_w"], np.float32),
        down_w=np.asarray(inputs["L0_attn_down_w"], np.float32),
        ffn_w=np.asarray(inputs["L0_ffn_weight"], np.float32),
        ffn_b=np.asarray(inputs["L0_ffn_bias"], np.float32),
        t_pdt=pat_dt, t_pdu=pat_du, t_yd=yd, t_alog=t_alog,
        t_idb=idb, t_idf=idf,
    )
    in_maps = []
    for core in range(8):
        m = dict(common)
        m["x"] = x[core * BC:(core + 1) * BC]
        in_maps.append(m)

    nc = build()
    res = run_bass_kernel_spmd(nc, in_maps, core_ids=list(range(8)))
    return np.concatenate([r["out"] for r in res.results], axis=0)


# version 09
# speedup vs baseline: 1.0478x; 1.0478x over previous
import numpy as np
import ml_dtypes
import concourse.bass as bass
import concourse.mybir as mybir
from concourse.ap import AP
from concourse.bass_utils import run_bass_kernel_spmd

F32 = mybir.dt.float32
BF16 = mybir.dt.bfloat16
AF = mybir.ActivationFunctionType
ALU = mybir.AluOpType
bf = ml_dtypes.bfloat16

BC, L, D = 2, 2048, 1024
H, DH, DS, DI = 16, 64, 48, 2048
NL, NT = H * DS, 6
DH_SUB, TC = 4, 512
NDHG, NTC = DH // DH_SUB, L // TC
CC = DH_SUB * TC
TA, NSUB = 512, L // 512
DT_BLK, DU_BLK = 128 * TC, 64 * TC
RND = 12582912.0
DBG = False
LN_EPS, RMS_EPS = 1e-5, 1e-6

LANE_H = np.array([l % 16 for l in range(128)])
ENGINES = ("q", "t", "a", "v", "g")
_ATTR = {"q": "sync", "t": "tensor", "a": "scalar", "v": "vector",
         "g": "gpsimd"}


class Sched:
    def __init__(self):
        self.ops = []
        self.cnt = {e: 0 for e in ENGINES}

    def op(self, eng, fn, waits=(), inc=1):
        ws = []
        for w in waits:
            if w is None:
                continue
            if isinstance(w, (list, tuple)) and w and isinstance(w[0], tuple):
                ws.extend(x for x in w if x is not None)
            else:
                ws.append(w)
        for (we, wv) in ws:
            if we == eng:
                assert wv <= self.cnt[eng], "self-wait on future op"
        self.ops.append((eng, fn, inc, ws))
        self.cnt[eng] += inc
        return (eng, self.cnt[eng])

    def dma(self, eng, fn, waits=()):
        return self.op(eng, fn, waits, inc=16)

    def render(self, block, sems):
        wvals = {e: set() for e in ENGINES}
        for (_, _, _, waits) in self.ops:
            for (we, wv) in waits:
                wvals[we].add(wv)
        for eng in ENGINES:
            bvals = sorted(wvals[eng]) if eng in ("q", "g") else []

            def body(o, eng=eng, bvals=bvals):
                waited = {e: 0 for e in ENGINES}
                cum = 0
                bi = 0
                selfw = 0
                for (e, fn, inc, waits) in self.ops:
                    if e != eng:
                        continue
                    while bi < len(bvals) and bvals[bi] <= cum:
                        bi += 1
                    if bi > 0 and bvals[bi - 1] > selfw:
                        o.wait_ge(sems[eng], bvals[bi - 1])
                        selfw = bvals[bi - 1]
                    for (we, wv) in waits:
                        if wv > waited[we]:
                            o.wait_ge(sems[we], wv)
                            waited[we] = wv
                    fn(o).then_inc(sems[eng], inc)
                    cum += inc
                if eng == "q":
                    for e2 in ENGINES:
                        if e2 != "q" and self.cnt[e2] > waited[e2]:
                            o.wait_ge(sems[e2], self.cnt[e2])
                    o.wait_ge(sems["q"], self.cnt["q"])
            getattr(block, _ATTR[eng])(body)


def host_tables():
    pat_dt = np.zeros((DH_SUB, 64, 128), np.float32)
    pat_du = np.zeros((DH_SUB, 32, 128), np.float32)
    for d in range(DH_SUB):
        p_ = d // 2
        for k in range(64):
            gg = p_ * 64 + k
            if gg // 32 == d:
                pat_dt[d, k, :] = (LANE_H == (gg % 32) // 2)
        for k in range(32):
            gg = p_ * 32 + k
            if gg // 16 == d:
                pat_du[d, k, :] = (LANE_H == gg % 16)
    yd = np.zeros((128, 16), np.float32)
    for ml in range(128):
        yd[ml, LANE_H[ml]] = 1.0
    pat_dt = pat_dt.transpose(1, 0, 2).reshape(64, DH_SUB * 128)
    pat_du = pat_du.transpose(1, 0, 2).reshape(32, DH_SUB * 128)
    return pat_dt.astype(bf), pat_du.astype(bf), yd.astype(bf)


def build():
    nc = bass.Bass("TRN2", target_bir_lowering=False)
    S = Sched()

    ein = lambda n, s, dt=F32: nc.dram_tensor(n, s, dt, kind="ExternalInput").ap()
    x_in = ein("x", [BC, L, D])
    norm1_w = ein("norm1_w", [D])
    norm2_w = ein("norm2_w", [D])
    dw_kernel = ein("dw_kernel", [D, 1, 4])
    dw_bias = ein("dw_bias", [D])
    pw_w = ein("pw_w", [D, D])
    pw_b = ein("pw_b", [D])
    xp_w = ein("xp_w", [160, DH])
    dtp_w = ein("dtp_w", [DH, 64])
    dtp_b = ein("dtp_b", [64])
    out_w = ein("out_w", [D, D])
    out_b = ein("out_b", [D])
    gate_w = ein("gate_w", [DI, D])
    up_w = ein("up_w", [DI, D])
    down_w = ein("down_w", [D, DI])
    ffn_w = ein("ffn_w", [D, D])
    ffn_b = ein("ffn_b", [D])
    t_pdt = ein("t_pdt", [64, DH_SUB * 128], BF16)
    t_pdu = ein("t_pdu", [32, DH_SUB * 128], BF16)
    t_yd = ein("t_yd", [128, 16], BF16)
    t_alog = ein("t_alog", [128, NT])
    t_idb = ein("t_idb", [128, 128], BF16)
    t_idf = ein("t_idf", [128, 128])

    out_t = nc.dram_tensor("out", [BC, L, D], F32, kind="ExternalOutput").ap()

    dint = lambda n, s, dt: nc.dram_tensor(n, s, dt, kind="Internal").ap()
    g_dt = dint("g_dt", [BC, NDHG, NTC, DT_BLK], BF16)
    g_du = dint("g_du", [BC, NDHG, NTC, DU_BLK], BF16)
    g_bt = dint("g_bt", [BC, NL, L], BF16)
    g_ct = dint("g_ct", [BC, NL, L], BF16)
    g_y = dint("g_y", [BC, D, L], BF16)
    g_xn = dint("g_xn", [BC, D, L], F32)
    g_xq = dint("g_xq", [BC, D, L], BF16)
    g_xsc = dint("g_xsc", [BC, L], F32)
    g_x1 = dint("g_x1", [BC, D, L], F32)
    g_h2 = dint("g_h2", [BC, D, L], BF16)
    g_mid = dint("g_mid", [BC, DI, L], BF16)
    g_pwT = dint("g_pwT", [D, D], BF16)
    g_opT = dint("g_opT", [D, D], BF16)
    g_gwT = dint("g_gwT", [D, DI], BF16)
    g_uwT = dint("g_uwT", [D, DI], BF16)
    g_dnT = dint("g_dnT", [DI, D], BF16)
    g_fqT = dint("g_fqT", [D, D], BF16)
    if DBG:
        g_dbg_s = dint("g_dbg_s", [BC, D, L], F32)
        g_dbg_g = dint("g_dbg_g", [BC, D, L], F32)

    PF_N, PB_N = 27, 40
    pf = nc.alloc_sbuf_tensor("pf", [128, PF_N * 1024], F32)
    pb = nc.alloc_sbuf_tensor("pb", [128, PB_N * 1024], BF16)

    def PF(i, c0=0, n=1024, r0=0, rows=128):
        return AP(pf, r0 * (PF_N * 1024) + i * 1024 + c0,
                  [[PF_N * 1024, rows], [1, n]])

    def PB(i, c0=0, n=1024, r0=0, rows=128):
        return AP(pb, r0 * (PB_N * 1024) + i * 1024 + c0,
                  [[PB_N * 1024, rows], [1, n]])

    sbt = lambda n, s, dt=F32: nc.alloc_sbuf_tensor(n, s, dt)
    idb_s = sbt("idb", [128, 128], BF16)
    idf_s = sbt("idf", [128, 128])
    colA = sbt("colA", [128, 15])
    colB = sbt("colB", [128, 9])
    kcol = sbt("kcol", [128, 32])
    c_pwb = sbt("c_pwb", [128, 8])
    c_n1 = sbt("c_n1", [128, 8])
    c_n2 = sbt("c_n2", [128, 8])
    c_ob = sbt("c_ob", [128, 8])
    c_fb = sbt("c_fb", [128, 8])
    c_ws = sbt("c_ws", [128, 8])
    c_dtb = sbt("c_dtb", [128, 1])
    c_dwb = sbt("c_dwb", [128, 8])
    xwT_s = sbt("xwT", [128, 160], BF16)
    dwT_s = sbt("dwT", [128, 64], BF16)
    s_dtx = sbt("s_dtx", [128, 8 * TA], BF16)
    s_bc = sbt("s_bc", [48, 2 * TA], BF16)
    s_acol = sbt("s_acol", [128, NT])
    s_pd = sbt("s_pd", [128, DH_SUB * 128], BF16)
    s_yd = sbt("s_yd", [128, 16], BF16)
    s_rd = sbt("s_rd", [128, 2 * TC], BF16)
    s_carry = sbt("s_carry", [128, NT * DH_SUB])
    s_tmp = sbt("s_tmp", [128, DH_SUB])
    s_yev = sbt("s_yev", [16, 2048], BF16)
    halo_b = sbt("halo", [128, 24])
    prepQ = sbt("prepQ", [128, 4])
    ones_c = sbt("ones_c", [128, 128], BF16)
    e0_s = sbt("e0_s", [128, 128], BF16)

    ps_mm = nc.alloc_psum_tensor("ps_mm", [128, 2048], F32)
    ps_t = nc.alloc_psum_tensor("ps_t", [128, 512], F32)
    ps_r = nc.alloc_psum_tensor("ps_r", [128, 512], F32)
    ps_y = nc.alloc_psum_tensor("ps_y", [128, 512], F32)
    ps_tb = nc.alloc_psum_tensor("ps_tb", [128, 512], BF16)

    def dma(dst, src, waits=(), slow=False):
        if slow:
            return S.dma("q", lambda o, d=dst, s=src: o.dma_start(
                out=d, in_=s, allow_slow_non_contiguous=True), waits)
        return S.dma("q", lambda o, d=dst, s=src: o.dma_start(out=d, in_=s),
                     waits)

    def gdma(dst, src, waits=()):
        return S.dma("g", lambda o, d=dst, s=src: o.dma_start(out=d, in_=s),
                     waits)

    def vcopy(dst, src, waits=()):
        return S.op("v", lambda o, d=dst, s=src: o.tensor_copy(out=d, in_=s),
                    waits)

    def vtt(dst, a0, a1, op, waits=()):
        return S.op("v", lambda o, d=dst, x=a0, y=a1, op=op:
                    o.tensor_tensor(out=d, in0=x, in1=y, op=op), waits)

    def gtt(dst, a0, a1, op, waits=()):
        return S.op("g", lambda o, d=dst, x=a0, y=a1, op=op:
                    o.tensor_tensor(out=d, in0=x, in1=y, op=op), waits)

    def vts(dst, src, s1, s2=None, op0=ALU.mult, op1=None, waits=()):
        if s2 is None:
            return S.op("v", lambda o, d=dst, s=src, s1=s1, op0=op0:
                        o.tensor_scalar(out=d, in0=s, scalar1=s1,
                                        scalar2=None, op0=op0), waits)
        return S.op("v", lambda o, d=dst, s=src, s1=s1, s2=s2, op0=op0,
                    op1=op1: o.tensor_scalar(out=d, in0=s, scalar1=s1,
                                             scalar2=s2, op0=op0, op1=op1),
                    waits)

    def vstt(dst, in0, scal, in1, op0, op1, waits=()):
        return S.op("v", lambda o, d=dst, i0=in0, sc=scal, i1=in1, op0=op0,
                    op1=op1: o.scalar_tensor_tensor(
                        out=d, in0=i0, scalar=sc, in1=i1, op0=op0, op1=op1),
                    waits)

    def vred(dst, src, op=ALU.add, absv=False, waits=()):
        if op == ALU.add:
            return S.op("v", lambda o, d=dst, s=src, ab=absv: o.reduce_sum(
                out=d, in_=s, axis=mybir.AxisListType.X,
                apply_absolute_value=ab), waits)
        return S.op("v", lambda o, d=dst, s=src, ab=absv: o.reduce_max(
            out=d, in_=s, axis=mybir.AxisListType.X,
            apply_absolute_value=ab), waits)

    def vrecip(dst, src, waits=()):
        return S.op("v", lambda o, d=dst, s=src: o.reciprocal(out=d, in_=s),
                    waits)

    def vmemset(dst, val, waits=()):
        return S.op("v", lambda o, d=dst, vv=val: o.memset(d, vv), waits)

    def vscan(dst, d0, d1, waits=()):
        return S.op("v", lambda o, d=dst, a0=d0, a1=d1: o.tensor_tensor_scan(
            out=d, data0=a0, data1=a1, initial=0.0, op0=ALU.mult,
            op1=ALU.add), waits)

    def act(dst, src, func, bias=0.0, scale=1.0, waits=()):
        return S.op("a", lambda o, d=dst, s=src, f=func, b=bias, sc=scale:
                    o.activation(out=d, in_=s, func=f, bias=b, scale=sc),
                    waits)

    def mm(out, lhsT, rhs, start, stop, waits=(), skip=False):
        return S.op("t", lambda o, ot=out, lh=lhsT, rh=rhs, st=start,
                    sp=stop, sk=skip: o.matmul(out=ot, lhsT=lh, rhs=rh,
                                               start=st, stop=sp,
                                               skip_group_check=sk), waits)

    def tT(out, src, ident, waits=()):
        return S.op("t", lambda o, ot=out, s=src, idn=ident:
                    o.transpose(out=ot, in_=s, identity=idn), waits)

    class Banks:
        def __init__(self):
            self.last = [None] * 4
            self.i = 0

        def get(self):
            b_ = self.i
            self.i = (b_ + 1) % 4
            return b_, ps_mm.ap()[:, b_ * 512:(b_ + 1) * 512], self.last[b_]

        def done(self, b_, m):
            self.last[b_] = m

        def drain(self):
            return [m for m in self.last if m is not None]

    BK = Banks()

    def load_dcol(dst_t, src_vec, waits=()):
        return dma(AP(dst_t, 0, [[dst_t.shape[1], 128], [1, 8]]),
                   AP(src_vec.tensor, 0, [[1, 128], [128, 8]]),
                   waits=waits, slow=True)

    m_idb = dma(idb_s.ap(), t_idb)
    m_idf = dma(idf_s.ap(), t_idf)
    m_pdt = dma(AP(s_pd, 0, [[DH_SUB * 128, 64], [1, DH_SUB * 128]]), t_pdt)
    m_pdu = dma(AP(s_pd, 64 * DH_SUB * 128, [[DH_SUB * 128, 32],
                                             [1, DH_SUB * 128]]), t_pdu)
    m_yd = dma(s_yd.ap(), t_yd)
    m_al = dma(s_acol.ap(), t_alog)
    m_ax = act(s_acol.ap(), s_acol.ap(), AF.Exp, waits=[m_al])
    m_acol = vts(s_acol.ap(), s_acol.ap(), -1.0, waits=[m_ax])
    m_on1 = vmemset(ones_c.ap(), 1.0)
    m_e0a = vmemset(e0_s.ap(), 0.0)
    m_on2 = vmemset(e0_s.ap()[0:1, :], 1.0, waits=[m_e0a])

    m_n1 = load_dcol(c_n1, norm1_w)
    m_n2 = load_dcol(c_n2, norm2_w)
    m_ob = load_dcol(c_ob, out_b)
    m_fb = load_dcol(c_fb, ffn_b)
    m_dwb = load_dcol(c_dwb, dw_bias)
    m_pbv = load_dcol(colB, pw_b)
    m_dtb = dma(AP(c_dtb, 0, [[1, 64], [1, 1]]),
                AP(dtp_b.tensor, 0, [[1, 64], [1, 1]]))
    m_kc = dma(AP(kcol, 0, [[32, 128], [4, 8], [1, 4]]),
               AP(dw_kernel.tensor, 0, [[4, 128], [512, 8], [1, 4]]))
    m_kf = None
    for i in range(8):
        m_kf = vts(kcol.ap()[:, 4 * i:4 * i + 4],
                   kcol.ap()[:, 4 * i:4 * i + 4],
                   c_n1.ap()[:, i:i + 1], waits=[m_kc, m_n1, m_kf])

    ps_tb_last = [None]
    pb_ev_last = [None, None]

    def prep_T(w_ap, gdst, O, K, quant=False, ws_col=None):
        mstore, mload = None, None
        for ot in range(O // 128):
            md = dma(PF(0, 0, K), w_ap[ot * 128:(ot + 1) * 128, 0:K],
                     waits=[mload])
            src = md
            if quant:
                m1 = vred(prepQ.ap()[:, 0:1], PF(0, 0, K), op=ALU.max,
                          absv=True, waits=[md])
                m2 = vrecip(prepQ.ap()[:, 1:2], prepQ.ap()[:, 0:1],
                            waits=[m1])
                m3 = vts(prepQ.ap()[:, 2:3], prepQ.ap()[:, 1:2], 127.0,
                         waits=[m2])
                S.op("v", lambda o, ot=ot, ws=ws_col: o.tensor_scalar(
                    out=ws.ap()[:, ot:ot + 1], in0=prepQ.ap()[:, 0:1],
                    scalar1=1.0 / 127.0, scalar2=None, op0=ALU.mult),
                    waits=[m1])
                m5 = vts(PF(0, 0, K), PF(0, 0, K), prepQ.ap()[:, 2:3],
                         waits=[m3])
                src = vts(PF(0, 0, K), PF(0, 0, K), RND, RND, ALU.add,
                          ALU.subtract, waits=[m5])
            mc = vcopy(PB(0, 0, K), PF(0, 0, K), waits=[src])
            mload = mc
            for g4 in range(K // 512):
                mts_ = []
                for j in range(4):
                    kt = g4 * 4 + j
                    mts_.append(tT(ps_tb.ap()[:, j * 128:(j + 1) * 128],
                                   PB(0, kt * 128, 128), idb_s.ap(),
                                   waits=[mc, m_idb,
                                          mts_[-1] if mts_
                                          else ps_tb_last[0]]))
                ev = g4 % 2
                me = act(PB(0, K + ev * 512, 512), ps_tb.ap()[:, 0:512],
                         AF.Copy, waits=[mts_[-1], pb_ev_last[ev]])
                ps_tb_last[0] = me
                for j in range(4):
                    kt = g4 * 4 + j
                    mstore = dma(AP(gdst.tensor,
                                    gdst.offset + kt * 128 * O + ot * 128,
                                    [[O, 128], [1, 128]]),
                                 PB(0, K + ev * 512 + j * 128, 128),
                                 waits=[me] if j == 0 else [])
                pb_ev_last[ev] = mstore
        return mstore

    m_pw = prep_T(pw_w, g_pwT, D, D)
    m_opw = prep_T(out_w, g_opT, D, D)
    m_gw = prep_T(gate_w, g_gwT, DI, D)
    m_uw = prep_T(up_w, g_uwT, DI, D)
    m_dn = prep_T(down_w, g_dnT, D, DI)
    m_fq = prep_T(ffn_w, g_fqT, D, D, quant=True, ws_col=c_ws)

    m0 = dma(PF(0, 0, 64), xp_w[0:128, :], waits=[m_fq])
    m1 = dma(PF(1, 0, 64, rows=32), xp_w[128:160, :])
    m2 = vcopy(PB(0, 0, 64), PF(0, 0, 64), waits=[m0, m_fq])
    m3 = vcopy(PB(1, 0, 64, rows=32), PF(1, 0, 64, rows=32), waits=[m1])
    m4 = tT(ps_tb.ap()[0:64, 0:128], PB(0, 0, 64), idb_s.ap(),
            waits=[m2, ps_tb_last[0]])
    m5 = act(AP(xwT_s, 0, [[160, 64], [1, 128]]), ps_tb.ap()[0:64, 0:128], AF.Copy,
             waits=[m4])
    m6 = tT(ps_tb.ap()[0:64, 128:160], PB(1, 0, 64, rows=32),
            idb_s.ap()[0:32, 0:32], waits=[m3, m5, ps_tb_last[0]])
    m7 = act(AP(xwT_s, 128, [[160, 64], [1, 32]]), ps_tb.ap()[0:64, 128:160], AF.Copy,
             waits=[m6])
    m8 = dma(PF(0, 0, 64, rows=64), dtp_w, waits=[m2])
    m9 = vcopy(PB(0, 0, 64, rows=64), PF(0, 0, 64, rows=64),
               waits=[m8, m3, m4])
    m10 = tT(ps_tb.ap()[0:64, 256:320], PB(0, 0, 64, rows=64),
             idb_s.ap()[0:64, 0:64], waits=[m9, m7, ps_tb_last[0]])
    m_dwT0 = act(AP(dwT_s, 0, [[64, 64], [1, 64]]),
                 ps_tb.ap()[0:64, 256:320], AF.Copy, waits=[m10])
    m_xwC = vcopy(AP(xwT_s, 64 * 160, [[160, 64], [1, 160]]),
                  AP(xwT_s, 0, [[160, 64], [1, 160]]), waits=[m5, m7])
    m_dwT = vcopy(AP(dwT_s, 64 * 64, [[64, 64], [1, 64]]),
                  AP(dwT_s, 0, [[64, 64], [1, 64]]), waits=[m_dwT0, m_xwC])
    ps_tb_last[0] = m_dwT0

    def load_pwT(waits):
        ms = []
        for kt in range(8):
            ms.append(dma(PB(29 + kt, 0, 1024),
                          AP(g_pwT.tensor, kt * 128 * D,
                             [[D, 128], [1, 1024]]), waits=waits))
        return ms

    m_pwT = load_pwT([m_pw, m_dwT])
    m_dwbc = vcopy(PB(28, 0, 8), c_dwb.ap()[:, 0:8], waits=[m_dwb, m9])
    m_bias = None
    for ot in range(8):
        mmsl = None
        for kt in range(8):
            mmsl = mm(ps_y.ap()[:, 0:1], PB(29 + kt, ot * 128, 128),
                      PB(28, kt, 1), start=(kt == 0), stop=(kt == 7),
                      waits=[m_pwT[kt], m_dwbc, m_bias, mmsl])
        m_bias = act(c_pwb.ap()[:, ot:ot + 1], ps_y.ap()[:, 0:1], AF.Copy,
                     waits=[mmsl])
    m_pwb = vtt(c_pwb.ap()[:, 0:8], c_pwb.ap()[:, 0:8], colB.ap()[:, 0:8],
                ALU.add, waits=[m_pbv, m_bias])

    m_phaseC_done = None

    for b in range(BC):
        if b > 0:
            m_pwT = load_pwT([m_phaseC_done])
        P = {}
        m_stage_last = None
        for sub in range(NSUB):
            t0 = sub * TA
            mx = []
            for i in range(4):
                mx.append(dma(PF(i),
                              x_in[b, t0 + i * 128:t0 + (i + 1) * 128, :],
                              waits=[P.get(("xn", i)), m_phaseC_done,
                                     m_pwb]))
            mxn = []
            for i in range(4):
                ms1 = vred(colA.ap()[:, 0:1], PF(i),
                           waits=[mx[i], P.get(("cA",))])
                msq = act(PF(8), PF(i), AF.Square,
                          waits=[mx[i], P.get(("sq",))])
                ms2 = vred(colA.ap()[:, 1:2], PF(8), waits=[msq])
                m_mu = vts(colA.ap()[:, 2:3], colA.ap()[:, 0:1], 1.0 / D,
                           waits=[ms1])
                m_m2 = vtt(colA.ap()[:, 3:4], colA.ap()[:, 2:3],
                           colA.ap()[:, 2:3], ALU.mult, waits=[m_mu])
                m_var = vstt(colA.ap()[:, 4:5], colA.ap()[:, 1:2], 1.0 / D,
                             colA.ap()[:, 3:4], ALU.mult, ALU.subtract,
                             waits=[ms2, m_m2])
                m_ep = vts(colA.ap()[:, 4:5], colA.ap()[:, 4:5], 1.0,
                           LN_EPS, ALU.mult, ALU.add, waits=[m_var])
                m_ln = act(colA.ap()[:, 5:6], colA.ap()[:, 4:5], AF.Ln,
                           waits=[m_ep])
                m_sc = vts(colA.ap()[:, 5:6], colA.ap()[:, 5:6], -0.5,
                           waits=[m_ln])
                m_rs = act(colA.ap()[:, 6:7], colA.ap()[:, 5:6], AF.Exp,
                           waits=[m_sc])
                mxn.append(vts(PF(i), PF(i), colA.ap()[:, 2:3],
                               colA.ap()[:, 6:7], ALU.subtract, ALU.mult,
                               waits=[m_rs, msq]))
                P[("sq",)] = msq
                P[("cA",)] = mxn[-1]
            m_xq = []
            for i in range(4):
                ma = vred(colA.ap()[:, 8:9], PF(i), op=ALU.max, absv=True,
                          waits=[mxn[i], P.get(("cB",))])
                mr = vrecip(colA.ap()[:, 9:10], colA.ap()[:, 8:9],
                            waits=[ma])
                mi = vts(colA.ap()[:, 10:11], colA.ap()[:, 9:10], 127.0,
                         waits=[mr])
                msc = vts(colA.ap()[:, 11:12], colA.ap()[:, 8:9],
                          1.0 / 127.0, waits=[ma])
                mdsc = dma(AP(g_xsc.tensor, b * L + t0 + i * 128,
                              [[1, 128], [1, 1]]),
                           colA.ap()[:, 11:12], waits=[msc])
                P[("cB",)] = mdsc
                mq1 = vts(PF(8), PF(i), colA.ap()[:, 10:11],
                          waits=[mi, P[("sq",)]])
                mq2 = vts(PF(8), PF(8), RND, RND, ALU.add, ALU.subtract,
                          waits=[mq1])
                m_xq.append(vcopy(PB(i), PF(8),
                                  waits=[mq2, P.get(("xq", i))]))
                P[("sq",)] = m_xq[-1]
                P[("xq", i)] = m_xq[-1]
            m_v = []
            for i in range(4):
                msq = act(PF(8), PF(i), AF.Square, waits=[mxn[i],
                                                          P[("sq",)]])
                ms = vred(colA.ap()[:, 12:13], PF(8),
                          waits=[msq, P.get(("cC",))])
                m_ep = vts(colA.ap()[:, 12:13], colA.ap()[:, 12:13],
                           1.0 / D, RMS_EPS, ALU.mult, ALU.add, waits=[ms])
                m_ln = act(colA.ap()[:, 13:14], colA.ap()[:, 12:13], AF.Ln,
                           waits=[m_ep])
                m_sc = vts(colA.ap()[:, 13:14], colA.ap()[:, 13:14], -0.5,
                           waits=[m_ln])
                m_rs = act(colA.ap()[:, 14:15], colA.ap()[:, 13:14], AF.Exp,
                           waits=[m_sc])
                m_v.append(vts(PF(4 + i), PF(i), colA.ap()[:, 14:15],
                               waits=[m_rs, P.get(("v", i))]))
                P[("sq",)] = msq
                P[("v", i)] = m_v[-1]
                P[("cC",)] = m_v[-1]
            for dt_ in range(8):
                mts = []
                for i in range(4):
                    mts.append(tT(ps_t.ap()[:, i * 128:(i + 1) * 128],
                                  PF(4 + i, dt_ * 128, 128), idf_s.ap(),
                                  waits=[m_v[i], m_idf, P.get(("ps_t",)),
                                         mts[-1] if mts else None]))
                m_ev = act(PF(11 + dt_, 3, 512), ps_t.ap()[:, 0:512],
                           AF.Copy, waits=[mts[-1], P.get(("haloS", dt_)),
                                           P.get(("dtS", dt_))])
                P[("ps_t",)] = m_ev
                if sub == 0:
                    m_hw = vmemset(PF(11 + dt_, 0, 3), 0.0, waits=[m_ev])
                else:
                    m_hw = vcopy(PF(11 + dt_, 0, 3),
                                 halo_b.ap()[:, dt_ * 3:dt_ * 3 + 3],
                                 waits=[m_ev, P.get(("haloS", dt_))])
                mc = vts(PF(19 + dt_, 0, 512), PF(11 + dt_, 0, 512),
                         kcol.ap()[:, dt_ * 4:dt_ * 4 + 1],
                         waits=[m_ev, m_hw, m_kf, P.get(("xh", dt_))])
                for k in range(1, 4):
                    if k == 3:
                        dst = PB(4 + dt_ // 2, (dt_ % 2) * 512, 512)
                        w_x = P.get(("ucb", dt_))
                    else:
                        dst = PF(19 + dt_, 0, 512)
                        w_x = None
                    mc = vstt(dst, PF(11 + dt_, k, 512),
                              kcol.ap()[:, dt_ * 4 + k:dt_ * 4 + k + 1],
                              PF(19 + dt_, 0, 512), ALU.mult, ALU.add,
                              waits=[mc, w_x])
                P[("ucb", dt_)] = mc
                P[("conv", dt_)] = mc
                m_hs = vcopy(halo_b.ap()[:, dt_ * 3:dt_ * 3 + 3],
                             PF(11 + dt_, TA, 3), waits=[mc])
                P[("haloS", dt_)] = m_hs
                mts2 = []
                for i in range(4):
                    mts2.append(tT(ps_t.ap()[:, i * 128:(i + 1) * 128],
                                   PF(i, dt_ * 128, 128), idf_s.ap(),
                                   waits=[mxn[i], P[("ps_t",)],
                                          mts2[-1] if mts2 else None]))
                m_ev2 = act(PF(8, 0, 512), ps_t.ap()[:, 0:512], AF.Copy,
                            waits=[mts2[-1], P[("sq",)]])
                P[("ps_t",)] = m_ev2
                m_st = dma(AP(g_xn.tensor, (b * D + dt_ * 128) * L + t0,
                              [[L, 128], [1, 512]]), PF(8, 0, 512),
                           waits=[m_ev2])
                P[("sq",)] = m_st
                P[("xn", 0)] = P[("xn", 1)] = P[("xn", 2)] = P[("xn", 3)] \
                    = m_st if dt_ == 7 else P.get(("xn", 0))
                mts3 = []
                for i in range(4):
                    mts3.append(tT(ps_tb.ap()[:, i * 128:(i + 1) * 128],
                                   PB(i, dt_ * 128, 128), idb_s.ap(),
                                   waits=[m_xq[i], ps_tb_last[0],
                                          mts3[-1] if mts3 else None]))
                m_ev3 = act(PB(38 + dt_ % 2, 0, 512), ps_tb.ap()[:, 0:512],
                            AF.Copy, waits=[mts3[-1],
                                            P.get(("xqS", dt_ % 2))])
                ps_tb_last[0] = m_ev3
                m_sq = dma(AP(g_xq.tensor, (b * D + dt_ * 128) * L + t0,
                              [[L, 128], [1, 512]]),
                           PB(38 + dt_ % 2, 0, 512), waits=[m_ev3])
                P[("xqS", dt_ % 2)] = m_sq
                if dt_ == 7:
                    for i in range(4):
                        P[("xq", i)] = m_sq
            for o8 in range(8):
                b_, bank, bfree = BK.get()
                mmsl = None
                for kt in range(8):
                    mmsl = mm(bank, PB(29 + kt, o8 * 128, 128),
                              PB(4 + kt // 2, (kt % 2) * 512, 512),
                              start=(kt == 0), stop=(kt == 7),
                              waits=[m_pwT[kt], P[("conv", kt)], bfree,
                                     mmsl])
                m_xh = act(PF(19 + o8, 0, 512), bank, AF.Identity,
                           bias=c_pwb.ap()[:, o8:o8 + 1],
                           waits=[mmsl, m_pwb, P[("conv", o8)]])
                BK.done(b_, m_xh)
                P[("xh", o8)] = m_xh
                m_xhb = vcopy(PB(8 + o8 // 2, (o8 % 2) * 512, 512),
                              PF(19 + o8, 0, 512),
                              waits=[m_xh, P.get(("xhb", o8))])
                P[("xhb", o8)] = m_xhb
            m_dtp = {}
            for h_ in range(16):
                hp, hl = h_ // 2, h_ % 2
                rhs = PB(8 + hp // 2, (hp % 2) * 512, 512, r0=64 * hl,
                         rows=64)
                w_rhs = P[("xhb", hp)]
                b1, bank1, bf1 = BK.get()
                mdtx = mm(AP(ps_mm, bank1.offset + (64 * hl) * 2048,
                             [[2048, 64], [1, 512]]),
                          AP(xwT_s, (64 * hl) * 160, [[160, 64], [1, 64]]),
                          rhs, True, True, waits=[w_rhs, m_xwC, bf1])
                m_dtxe = act(AP(s_dtx, (64 * hl) * (8 * TA) + hp * TA,
                                [[8 * TA, 64], [1, TA]]),
                             AP(ps_mm, bank1.offset + (64 * hl) * 2048,
                                [[2048, 64], [1, 512]]),
                             AF.Copy,
                             waits=[mdtx, P.get(("dtx", h_))])
                BK.done(b1, m_dtxe)
                P[("dtx", h_)] = m_dtxe
                b2, bank2, bf2 = BK.get()
                mB = mm(AP(ps_mm, bank2.offset + (64 * hl) * 2048,
                           [[2048, 48], [1, 512]]),
                        AP(xwT_s, (64 * hl) * 160 + 64, [[160, 64], [1, 48]]),
                        rhs, True, True, waits=[w_rhs, m_xwC, bf2])
                m_Be = act(AP(s_bc, (0) * TA if hl == 0 else 0, [[2 * TA, 48], [1, TA]])
                           if hl == 0 else AP(s_bc, 0, [[2 * TA, 48], [1, TA]]),
                           AP(ps_mm, bank2.offset + (64 * hl) * 2048,
                              [[2048, 48], [1, 512]]),
                           AF.Copy, waits=[mB, P.get(("bcS", 0))])
                BK.done(b2, m_Be)
                m_Bs = dma(AP(g_bt.tensor, (b * NL + h_) * L + t0,
                              [[16 * L, 48], [1, 512]]),
                           s_bc.ap()[:, 0:TA], waits=[m_Be])
                P[("bcS", 0)] = m_Bs
                b3, bank3, bf3 = BK.get()
                mC = mm(AP(ps_mm, bank3.offset + (64 * hl) * 2048,
                           [[2048, 48], [1, 512]]),
                        AP(xwT_s, (64 * hl) * 160 + 112, [[160, 64], [1, 48]]),
                        rhs, True, True, waits=[w_rhs, m_xwC, bf3])
                m_Ce = act(AP(s_bc, TA, [[2 * TA, 48], [1, TA]]),
                           AP(ps_mm, bank3.offset + (64 * hl) * 2048,
                              [[2048, 48], [1, 512]]),
                           AF.Copy, waits=[mC, P.get(("bcS2", 0))])
                BK.done(b3, m_Ce)
                m_Cs = dma(AP(g_ct.tensor, (b * NL + h_) * L + t0,
                              [[16 * L, 48], [1, 512]]),
                           s_bc.ap()[:, TA:2 * TA], waits=[m_Ce])
                P[("bcS2", 0)] = m_Cs
                m_stage_last = m_Cs
                b4, bank4, bf4 = BK.get()
                mdt = mm(AP(ps_mm, bank4.offset + (64 * hl) * 2048,
                            [[2048, 64], [1, 512]]),
                         AP(dwT_s, (64 * hl) * 64, [[64, 64], [1, 64]]),
                         AP(s_dtx, (64 * hl) * (8 * TA) + hp * TA,
                            [[8 * TA, 64], [1, TA]]), True, True,
                         waits=[m_dtxe, m_dwT, bf4])
                m_dte = act(PF(11 + hp, 0, 512, r0=64 * hl, rows=64),
                            AP(ps_mm, bank4.offset + (64 * hl) * 2048,
                               [[2048, 64], [1, 512]]),
                            AF.Identity,
                            bias=c_dtb.ap()[0:64, 0:1],
                            waits=[mdt, m_dtb, P[("conv", hp)],
                                   P[("haloS", hp)], P.get(("dtS", hp))])
                BK.done(b4, m_dte)
                m_dtp[h_] = m_dte
            for j in range(8):
                wj = [m_dtp[2 * j], m_dtp[2 * j + 1]]
                mab = act(PF(8, 0, 512), PF(11 + j, 0, 512), AF.Abs,
                          waits=wj + [P[("sq",)]])
                mex = act(PF(8, 0, 512), PF(8, 0, 512), AF.Exp, scale=-1.0,
                          waits=[mab])
                mln = act(PF(8, 0, 512), PF(8, 0, 512), AF.Ln, bias=1.0,
                          waits=[mex])
                mre = act(PF(9, 0, 512), PF(11 + j, 0, 512), AF.Relu,
                          waits=wj + [P.get(("sp9",))])
                mdt2 = vtt(PF(11 + j, 0, 512), PF(9, 0, 512), PF(8, 0, 512),
                           ALU.add, waits=[mln, mre] + wj)
                P[("sq",)] = mdt2
                P[("sp9",)] = mdt2
                mhi = vcopy(PB(12 + j // 2, (j % 2) * 512, 512),
                            PF(11 + j, 0, 512),
                            waits=[mdt2, P.get(("dthi", j))])
                mlo1 = vtt(PF(9, 0, 512), PF(11 + j, 0, 512),
                           PB(12 + j // 2, (j % 2) * 512, 512),
                           ALU.subtract, waits=[mhi])
                mlo = vcopy(PB(16 + j // 2, (j % 2) * 512, 512),
                            PF(9, 0, 512),
                            waits=[mlo1, P.get(("dtlo", j))])
                mdu = vtt(PB(20 + j // 2, (j % 2) * 512, 512),
                          PF(11 + j, 0, 512), PF(19 + j, 0, 512), ALU.mult,
                          waits=[mdt2, P[("xh", j)], P.get(("dtub", j))])
                P[("dthi", j)] = mhi
                P[("dtlo", j)] = mlo
                P[("dtub", j)] = mdu
                P[("sp9",)] = mlo
                P[("xh", j)] = mdu
                for hl in range(2):
                    h_ = 2 * j + hl
                    for x_ in range(2):
                        tile_i = (12 if x_ == 0 else 16) + j // 2
                        src_ap = AP(pb, (64 * hl) * (PB_N * 1024)
                                    + tile_i * 1024 + (j % 2) * 512,
                                    [[PB_N * 1024, 64], [1, TC]])
                        m_s = dma(
                            AP(g_dt.tensor,
                               (b * NDHG * NTC + sub) * DT_BLK
                               + (h_ * 2 + x_) * TC,
                               [[NTC * DT_BLK, NDHG], [32 * TC, DH_SUB],
                                [1, TC]]),
                            src_ap, waits=[mhi if x_ == 0 else mlo])
                        m_stage_last = m_s
                    src_ap = AP(pb, (64 * hl) * (PB_N * 1024)
                                + (20 + j // 2) * 1024 + (j % 2) * 512,
                                [[PB_N * 1024, 64], [1, TC]])
                    m_s = dma(
                        AP(g_du.tensor,
                           (b * NDHG * NTC + sub) * DU_BLK + h_ * TC,
                           [[NTC * DU_BLK, NDHG], [16 * TC, DH_SUB],
                            [1, TC]]),
                        src_ap, waits=[mdu])
                    m_stage_last = m_s
                P[("dtS", j)] = m_stage_last

        m_bt = dma(AP(pb, 0, [[PB_N * 1024, 128], [1, NT * L]]),
                   AP(g_bt.tensor, b * NL * L,
                      [[L, 128], [128 * L, NT], [1, L]]),
                   waits=[m_stage_last, P[("xq", 0)], P[("bcS", 0)],
                          P[("bcS2", 0)], P[("xqS", 0)], P[("xqS", 1)]])
        m_ct = dma(AP(pb, 12 * 1024, [[PB_N * 1024, 128], [1, NT * L]]),
                   AP(g_ct.tensor, b * NL * L,
                      [[L, 128], [128 * L, NT], [1, L]]),
                   waits=[m_stage_last])
        chunks = [(dhg, tc_) for dhg in range(NDHG) for tc_ in range(NTC)]
        NCH = len(chunks)
        m_repmm = [None] * NCH
        m_dumm = [None] * NCH
        m_duev = [None] * NCH
        m_exps = [None] * NCH
        m_rdt = [None] * NCH
        m_rdu = [None] * NCH
        m_abuf = {}
        m_abuf0 = [None]

        def a_tile(Pp_, ti_):
            if ti_ == 0:
                return 0
            return 2 * ti_ if Pp_ == 0 else 10 + 2 * ti_
        m_wfree = [None, None]
        m_wlast = [None, None]
        m_mfree = [None, None]
        scan_st = {"h": None, "yevl": None}
        m_carry = [None] * NT
        m_ydma = [None] * 4
        YQ = [(ps_y, 0), (ps_y, 32), (ps_r, 0), (ps_r, 32)]

        def chunk_prep(k):
            dhg, tc_ = chunks[k]
            Pp = k % 2
            blk = (b * NDHG + dhg) * NTC + tc_
            w_pre = [m_bt, m_ct] if k == 0 else []
            mrdt_n = []
            for p_ in range(2):
                mrdt_n.append(dma(
                    AP(s_rd, p_ * TC, [[2 * TC, 64], [1, TC]]),
                    AP(g_dt.tensor, blk * DT_BLK + p_ * 64 * TC,
                       [[TC, 64], [1, TC]]),
                    waits=[m_repmm[k - 1] if k else None,
                           m_rdt[k - 1] if k else None] + w_pre))
            m_rdt[k] = mrdt_n[-1]
            mrdu_n = []
            for p_ in range(2):
                mrdu_n.append(dma(
                    AP(s_rd, 64 * 2 * TC + p_ * TC, [[2 * TC, 32], [1, TC]]),
                    AP(g_du.tensor, blk * DU_BLK + p_ * 32 * TC,
                       [[TC, 32], [1, TC]]),
                    waits=[m_dumm[k - 1] if k else None,
                           m_rdu[k - 1] if k else None]))
            m_rdu[k] = mrdu_n[-1]
            mmx = m_duev[k - 1] if k else None
            for d in range(DH_SUB):
                p_ = d // 2
                mmx = mm(ps_mm.ap()[:, d * TC:(d + 1) * TC],
                         AP(s_pd, d * 128, [[DH_SUB * 128, 64], [1, 128]]),
                         AP(s_rd, p_ * TC, [[2 * TC, 64], [1, TC]]),
                         True, True, waits=[mrdt_n[p_], m_pdt, mmx])
            m_repmm[k] = mmx
            m_exp = []
            for ti in range(NT):
                m_exp.append(act(PF(a_tile(Pp, ti), 0, 2048), ps_mm.ap(),
                                 AF.Exp, scale=s_acol.ap()[:, ti:ti + 1],
                                 waits=[m_repmm[k], m_acol,
                                        m_abuf0[0] if ti == 0
                                        else m_abuf.get((Pp, ti))]))
            m_exps[k] = m_exp
            mmx = m_exp[-1]
            for d in range(DH_SUB):
                p_ = d // 2
                mmx = mm(ps_mm.ap()[:, d * TC:(d + 1) * TC],
                         AP(s_pd, 64 * DH_SUB * 128 + d * 128,
                            [[DH_SUB * 128, 32], [1, 128]]),
                         AP(s_rd, 64 * 2 * TC + p_ * TC,
                            [[2 * TC, 32], [1, TC]]),
                         True, True, waits=[mrdu_n[p_], m_pdu, mmx])
            m_dumm[k] = mmx
            m_duev[k] = act(PB(24 + 2 * Pp, 0, 2048), ps_mm.ap(), AF.Copy,
                            waits=[m_dumm[k], m_wlast[Pp]])

        h_free = [None, None]

        def chunk_tiles(k, prep_next=None):
            dhg, tc_ = chunks[k]
            Pp = k % 2
            du_i = 24 + 2 * Pp
            m_ws = [None] * NT

            def emit_w(ti):
                pp = ti % 2
                m_ws[ti] = gtt(
                    PB(28 + pp * 2, 0, 2048).rearrange(
                        "p (d t) -> p d t", d=DH_SUB),
                    PB(du_i, 0, 2048).rearrange("p (d t) -> p d t",
                                                d=DH_SUB),
                    AP(pb, ti * L + tc_ * TC,
                       [[PB_N * 1024, 128], [0, DH_SUB], [1, TC]]),
                    ALU.mult,
                    waits=[m_duev[k], m_bt, m_wfree[pp]])
                m_wlast[Pp] = m_ws[ti]

            emit_w(0)
            emit_w(1)
            for ti in range(NT):
                if ti == 1 and prep_next is not None:
                    prep_next()
                pp = ti % 2
                a_i = a_tile(Pp, ti)
                w_i = 28 + pp * 2
                m_i = 32 + pp * 2
                h_i = 22 + 2 * (ti % 2)
                a_seg = AP(pf, a_i * 1024, [[PF_N * 1024, 128],
                                            [TC, DH_SUB]])
                w_seg = AP(pb, w_i * 1024, [[PB_N * 1024, 128],
                                            [TC, DH_SUB]])
                m_w = m_ws[ti]
                cs = s_carry.ap()[:, ti * DH_SUB:(ti + 1) * DH_SUB]
                if tc_ == 0:
                    m_fx = vmemset(a_seg, 0.0, waits=[m_exps[k][ti]])
                else:
                    m_t1 = vtt(s_tmp.ap(), a_seg, cs, ALU.mult,
                               waits=[m_exps[k][ti], m_carry[ti]])
                    m_t2 = m_t1
                    for d in range(DH_SUB):
                        wcol = AP(pb, w_i * 1024 + d * TC,
                                  [[PB_N * 1024, 128], [1, 1]])
                        m_t2 = vtt(wcol, wcol,
                                   s_tmp.ap()[:, d:d + 1], ALU.add,
                                   waits=[m_t2, m_w if d == 0 else None])
                    m_fx = vmemset(a_seg, 0.0, waits=[m_t2])
                m_scan = vscan(PF(h_i, 0, 2048), PF(a_i, 0, 2048),
                               PB(w_i, 0, 2048),
                               waits=[m_fx, m_w, h_free[ti % 2]])
                if ti == 0:
                    m_abuf0[0] = m_scan
                else:
                    m_abuf[(Pp, ti)] = m_scan
                m_wfree[pp] = m_scan
                if ti + 2 < NT:
                    emit_w(ti + 2)
                m_cw = vcopy(cs, AP(pf, h_i * 1024 + TC - 1,
                                    [[PF_N * 1024, 128], [TC, DH_SUB]]),
                             waits=[m_scan])
                m_carry[ti] = m_cw
                m_m = vtt(
                    PB(m_i, 0, 2048).rearrange("p (d t) -> p d t",
                                               d=DH_SUB),
                    PF(h_i, 0, 2048).rearrange("p (d t) -> p d t",
                                               d=DH_SUB),
                    AP(pb, 12 * 1024 + ti * L + tc_ * TC,
                       [[PB_N * 1024, 128], [0, DH_SUB], [1, TC]]),
                    ALU.mult,
                    waits=[m_scan, m_ct, m_mfree[pp]])
                h_free[ti % 2] = [m_m, m_cw]
                for qt in range(4):
                    pq, r0 = YQ[qt]
                    m_yq = mm(AP(pq, r0 * 512, [[512, 16], [1, 512]]),
                              s_yd.ap(), PB(m_i, qt * 512, 512),
                              start=(ti == 0), stop=(ti == NT - 1),
                              waits=[m_m, m_yd,
                                     scan_st["yevl"] if ti == 0 else None],
                              skip=True)
                    m_mfree[pp] = m_yq
                if ti == NT - 1:
                    m_evs = []
                    for qt in range(4):
                        pq, r0 = YQ[qt]
                        m_evs.append(act(
                            AP(s_yev, qt * 512, [[2048, 16], [1, 512]]),
                            AP(pq, r0 * 512, [[512, 16], [1, 512]]),
                            AF.Copy,
                            waits=[m_mfree[pp], m_ydma[qt]]))
                    scan_st["yevl"] = m_evs[-1]
                    for qt in range(4):
                        m_ydma[qt] = dma(
                            AP(g_y.tensor,
                               (b * D + dhg * DH_SUB + qt) * L + tc_ * TC,
                               [[64 * L, 16], [1, TC]]),
                            AP(s_yev, qt * 512, [[2048, 16], [1, 512]]),
                            waits=[m_evs[qt]])

        chunk_prep(0)
        for k in range(NCH):
            chunk_tiles(k, prep_next=(lambda kk=k + 1: chunk_prep(kk))
                        if k + 1 < NCH else None)
        m_scan_done = [m_ydma[qt] for qt in range(4)]
        m_hbuf = [x for x in h_free[0] + h_free[1] if x is not None]

        m_opTs = []
        for kt in range(8):
            m_opTs.append(dma(PB(kt, 0, 1024),
                              AP(g_opT.tensor, kt * 128 * D,
                                 [[D, 128], [1, 1024]]),
                              waits=[m_hbuf, m_wlast[0], m_wlast[1],
                                     m_mfree[0], m_mfree[1], m_duev[-1],
                                     scan_st["yevl"]] + m_scan_done))
        PC = {}
        m_h2S = {}
        m_x1S = {}
        for sub in range(NSUB):
            t0 = sub * TA
            m_yb = []
            for j in range(8):
                m_yb.append(dma(PB(32 + j // 2, (j % 2) * 512, 512),
                                AP(g_y.tensor, (b * D + j * 128) * L + t0,
                                   [[L, 128], [1, 512]]),
                                waits=[PC.get(("yb", j)),
                                       PC.get(("ybs", j))]
                                + ([m_mfree[0], m_mfree[1], m_hbuf]
                                   + m_scan_done if sub == 0 else [])))
            m_mm_last = []
            for o8 in range(8):
                b_, bank, bfree = BK.get()
                mmsl = None
                for kt in range(8):
                    mmsl = mm(bank, PB(kt, o8 * 128, 128),
                              PB(32 + kt // 2, (kt % 2) * 512, 512),
                              start=(kt == 0), stop=(kt == 7),
                              waits=[m_opTs[kt], m_yb[kt], bfree, mmsl])
                m_mm_last.append(mmsl)
                m_oe = act(PF(8 + o8, 0, 512), bank, AF.Identity,
                           bias=c_ob.ap()[:, o8:o8 + 1],
                           waits=[mmsl, m_ob, PC.get(("sq", o8))])
                BK.done(b_, m_oe)
                m_xl = dma(PF(16 + o8, 0, 512),
                           AP(g_xn.tensor, (b * D + o8 * 128) * L + t0,
                              [[L, 128], [1, 512]]),
                           waits=[PC.get(("xn", o8))]
                           + ([m_hbuf] if sub == 0 else []))
                m_x1 = vtt(PF(o8, 0, 512), PF(8 + o8, 0, 512),
                           PF(16 + o8, 0, 512), ALU.add,
                           waits=[m_oe, m_xl, PC.get(("x1s", o8)),
                                  PC.get(("x1", o8))])
                PC[("x1", o8)] = m_x1
                PC[("xn", o8)] = m_x1
                m_x1s = dma(AP(g_x1.tensor, (b * D + o8 * 128) * L + t0,
                               [[L, 128], [1, 512]]),
                            PF(o8, 0, 512), waits=[m_x1])
                m_x1S[(sub, o8)] = m_x1s
                PC[("x1s", o8)] = m_x1s
            for o8 in range(8):
                m_sq = act(PF(8 + o8, 0, 512), PF(o8, 0, 512), AF.Square,
                           waits=[PC[("x1", o8)], PC.get(("sq", o8))])
                PC[("sq", o8)] = m_sq
                m_hi = vcopy(PB(36 + o8 // 2, (o8 % 2) * 512, 512),
                             PF(8 + o8, 0, 512),
                             waits=[m_sq, PC.get(("sqh", o8))])
                m_lo1 = vtt(PF(8 + o8, 0, 512), PF(8 + o8, 0, 512),
                            PB(36 + o8 // 2, (o8 % 2) * 512, 512),
                            ALU.subtract, waits=[m_hi])
                m_lo = vcopy(PB(32 + o8 // 2, (o8 % 2) * 512, 512),
                             PF(8 + o8, 0, 512),
                             waits=[m_lo1, m_mm_last[-1],
                                    PC.get(("sql", o8))])
                PC[("sqh", o8)] = m_hi
                PC[("sql", o8)] = m_lo
                PC[("sq", o8)] = m_lo
            mmsl = PC.get(("msrow",))
            first = True
            for j in range(8):
                for which in range(2):
                    t_i = (36 if which == 0 else 32) + j // 2
                    mmsl = mm(ps_y.ap(), ones_c.ap(),
                              PB(t_i, (j % 2) * 512, 512),
                              start=first, stop=(j == 7 and which == 1),
                              waits=[PC[("sqh", j)] if which == 0
                                     else PC[("sql", j)],
                                     m_on1, mmsl])
                    first = False
            m_pre = vts(PF(24, 0, 512), ps_y.ap(), 1.0 / D, RMS_EPS,
                        ALU.mult, ALU.add,
                        waits=[mmsl, PC.get(("msrow2",)), PC.get(("rowb",))])
            m_ms = act(PF(24, 0, 512), PF(24, 0, 512), AF.Ln,
                       waits=[m_pre])
            PC[("msrow",)] = m_ms
            m_sc2 = vts(PF(24, 0, 512), PF(24, 0, 512), -0.5, waits=[m_ms])
            m_rep = act(PF(24, 0, 512), PF(24, 0, 512), AF.Exp,
                       waits=[m_sc2])
            PC[("rowb",)] = m_rep
            PC[("msrow2",)] = m_rep
            for o8 in range(8):
                m_h2 = vstt(PB(32 + o8 // 2, (o8 % 2) * 512, 512),
                            PF(o8, 0, 512), c_n2.ap()[:, o8:o8 + 1],
                            PF(24, 0, 512), ALU.mult, ALU.mult,
                            waits=[PC[("x1", o8)], m_rep, m_n2])
                PC[("yb", o8)] = m_h2
                PC[("h2rep",)] = m_h2
                m_h2s = dma(AP(g_h2.tensor, (b * D + o8 * 128) * L + t0,
                               [[L, 128], [1, 512]]),
                            PB(32 + o8 // 2, (o8 % 2) * 512, 512),
                            waits=[m_h2])
                m_h2S[(sub, o8)] = m_h2s
                PC[("ybs", o8)] = m_h2s

        m_gws, m_uws = [], []
        for kt in range(8):
            m_gws.append(dma(
                AP(pb, (2 * kt) * 1024, [[PB_N * 1024, 128], [1, 2048]]),
                AP(g_gwT.tensor, kt * 128 * DI, [[DI, 128], [1, 2048]]),
                waits=[m_h2S[(NSUB - 1, 7)], m_opTs[kt]]))
        for kt in range(8):
            m_uws.append(dma(
                AP(pb, (16 + 2 * kt) * 1024, [[PB_N * 1024, 128], [1, 2048]]),
                AP(g_uwT.tensor, kt * 128 * DI, [[DI, 128], [1, 2048]]),
                waits=[m_h2S[(NSUB - 1, 7)]]))
        PC2 = {}
        m_midS = {}
        for sub in range(NSUB):
            t0 = sub * TA
            m_h2l = []
            for j in range(8):
                m_h2l.append(dma(PB(32 + j // 2, (j % 2) * 512, 512),
                                 AP(g_h2.tensor, (b * D + j * 128) * L + t0,
                                    [[L, 128], [1, 512]]),
                                 waits=[PC2.get(("h2", j)),
                                        m_h2S[(sub, j)]] +
                                 ([m_uws[-1]] if sub == 0 else [])))
                PC2[("h2", j)] = m_h2l[-1]
            for o16 in range(16):
                bg, bankg, bfg = BK.get()
                mmsl = None
                for kt in range(8):
                    mmsl = mm(bankg,
                              AP(pb, (2 * kt) * 1024 + o16 * 128,
                                 [[PB_N * 1024, 128], [1, 128]]),
                              PB(32 + kt // 2, (kt % 2) * 512, 512),
                              start=(kt == 0), stop=(kt == 7),
                              waits=[m_gws[kt], m_h2l[kt], bfg, mmsl])
                m_sg = act(PB(36 + o16 % 2, 0, 512), bankg, AF.Silu,
                           waits=[mmsl, PC2.get(("sgb", o16 % 2))])
                BK.done(bg, m_sg)
                bu, banku, bfu = BK.get()
                mmsl = None
                for kt in range(8):
                    mmsl = mm(banku,
                              AP(pb, (16 + 2 * kt) * 1024 + o16 * 128,
                                 [[PB_N * 1024, 128], [1, 128]]),
                              PB(32 + kt // 2, (kt % 2) * 512, 512),
                              start=(kt == 0), stop=(kt == 7),
                              waits=[m_uws[kt], m_h2l[kt], bfu, mmsl])
                m_ue = act(PB(38 + o16 % 2, 0, 512), banku, AF.Copy,
                           waits=[mmsl, PC2.get(("ueb", o16 % 2))])
                BK.done(bu, m_ue)
                m_mid = vtt(PB(36 + o16 % 2, 0, 512),
                            PB(36 + o16 % 2, 0, 512),
                            PB(38 + o16 % 2, 0, 512), ALU.mult,
                            waits=[m_sg, m_ue])
                PC2[("ueb", o16 % 2)] = m_mid
                m_ms = dma(AP(g_mid.tensor, (b * DI + o16 * 128) * L + t0,
                              [[L, 128], [1, 512]]),
                           PB(36 + o16 % 2, 0, 512), waits=[m_mid])
                PC2[("sgb", o16 % 2)] = m_ms
                m_midS[(sub, o16)] = m_ms

        m_dns, m_fqs = [], []
        for kt in range(16):
            m_dns.append(dma(PB(kt, 0, 1024),
                             AP(g_dnT.tensor, kt * 128 * D,
                                [[D, 128], [1, 1024]]),
                             waits=[m_midS[(NSUB - 1, 15)],
                                    (m_gws + m_uws)[kt]]))
        for kt in range(8):
            m_fqs.append(dma(PB(16 + kt, 0, 1024),
                             AP(g_fqT.tensor, kt * 128 * D,
                                [[D, 128], [1, 1024]]),
                             waits=[m_midS[(NSUB - 1, 15)],
                                    m_uws[kt]]))
        PC3 = {}
        m_out_last = None
        for sub in range(NSUB):
            t0 = sub * TA
            m_midl = []
            for j in range(16):
                m_midl.append(dma(PB(24 + j // 2, (j % 2) * 512, 512),
                                  AP(g_mid.tensor,
                                     (b * DI + j * 128) * L + t0,
                                     [[L, 128], [1, 512]]),
                                  waits=[PC3.get(("mid", j)),
                                         m_midS[(sub, j)]] +
                                  ([m_dns[-1], m_fqs[-1]] if sub == 0
                                   else [])))
                PC3[("mid", j)] = m_midl[-1]
            m_xql = []
            for j in range(8):
                m_xql.append(dma(PB(32 + j // 2, (j % 2) * 512, 512),
                                 AP(g_xq.tensor, (b * D + j * 128) * L + t0,
                                    [[L, 128], [1, 512]]),
                                 waits=[PC3.get(("xq", j))] +
                                 ([m_fqs[-1]] if sub == 0 else [])))
                PC3[("xq", j)] = m_xql[-1]
            if sub == 0:
                m_z36 = vmemset(PB(36, 0, 512), 0.0,
                                waits=[PC2.get(("sgb", 0)),
                                       PC2.get(("ueb", 0))])
                PC3[("z36",)] = m_z36
            m_xsl = dma(PF(26, 0, 512, rows=1),
                        AP(g_xsc.tensor, b * L + t0, [[TA, 1], [1, TA]]),
                        waits=[PC3.get(("xsc",)), PC.get(("rowb",))])
            m_xsb = vcopy(PB(36, 0, 512, rows=1), PF(26, 0, 512, rows=1),
                          waits=[m_xsl, PC3.get(("xscb",)),
                                 PC3.get(("z36",))])
            m_xrep = mm(ps_r.ap(), e0_s.ap(), PB(36, 0, 512),
                        True, True,
                        waits=[m_xsb, m_on2, PC3.get(("t2rep",))])
            PC3[("xsc",)] = m_xsb
            PC3[("xscb",)] = m_xrep
            for o8 in range(8):
                bd, bankd, bfd = BK.get()
                mmsl = None
                for kt in range(16):
                    mmsl = mm(bankd, PB(kt, o8 * 128, 128),
                              PB(24 + kt // 2, (kt % 2) * 512, 512),
                              start=(kt == 0), stop=(kt == 15),
                              waits=[m_dns[kt], m_midl[kt], bfd, mmsl])
                m_de = act(PF(8 + o8, 0, 512), bankd, AF.Copy,
                           waits=[mmsl, PC3.get(("pf8", o8))])
                BK.done(bd, m_de)
                m_x1l = dma(PF(16 + o8, 0, 512),
                            AP(g_x1.tensor, (b * D + o8 * 128) * L + t0,
                               [[L, 128], [1, 512]]),
                            waits=[PC3.get(("x1", o8)), m_x1S[(sub, o8)]])
                m_s = vtt(PF(o8, 0, 512), PF(8 + o8, 0, 512),
                          PF(16 + o8, 0, 512), ALU.add,
                          waits=[m_de, m_x1l, PC3.get(("s", o8))])
                PC3[("s", o8)] = m_s
                bq, bankq, bfq = BK.get()
                mmsl = None
                for kt in range(8):
                    mmsl = mm(bankq, PB(16 + kt, o8 * 128, 128),
                              PB(32 + kt // 2, (kt % 2) * 512, 512),
                              start=(kt == 0), stop=(kt == 7),
                              waits=[m_fqs[kt], m_xql[kt], bfq, mmsl])
                m_t1 = act(PF(16 + o8, 0, 512), bankq, AF.Identity,
                           scale=c_ws.ap()[:, o8:o8 + 1],
                           waits=[mmsl, m_s, m_x1l])
                BK.done(bq, m_t1)
                PC3[("x1", o8)] = m_t1
                m_t2 = vtt(PF(16 + o8, 0, 512), PF(16 + o8, 0, 512),
                           ps_r.ap(), ALU.mult,
                           waits=[m_t1, m_xrep])
                PC3[("t2rep",)] = m_t2
                m_g = act(PF(8 + o8, 0, 512), PF(16 + o8, 0, 512), AF.Silu,
                          bias=c_fb.ap()[:, o8:o8 + 1],
                          waits=[m_t2, m_fb, m_de])
                m_dbg = None
                if DBG:
                    m_d1 = dma(AP(g_dbg_s.tensor,
                                  (b * D + o8 * 128) * L + t0,
                                  [[L, 128], [1, 512]]),
                               PF(o8, 0, 512), waits=[m_s])
                    m_d2 = dma(AP(g_dbg_g.tensor,
                                  (b * D + o8 * 128) * L + t0,
                                  [[L, 128], [1, 512]]),
                               PF(8 + o8, 0, 512), waits=[m_g])
                    m_dbg = [m_d1, m_d2]
                m_sg2 = vtt(PF(o8, 0, 512), PF(o8, 0, 512),
                            PF(8 + o8, 0, 512), ALU.mult,
                            waits=[m_g, m_s, m_dbg])
                PC3[("pf8", o8)] = m_sg2
                PC3[("s", o8)] = m_sg2
            for i in range(4):
                mts = []
                for half in range(2):
                    for j in range(4):
                        o8 = half * 4 + j
                        mts.append(tT(ps_t.ap()[:, j * 128:(j + 1) * 128],
                                      PF(o8, i * 128, 128), idf_s.ap(),
                                      waits=[PC3[("s", o8)],
                                             PC3.get(("ps_t",)),
                                             mts[-1] if mts else None]))
                    m_ev = act(PF(24, half * 512, 512),
                               ps_t.ap()[:, 0:512], AF.Copy,
                               waits=[mts[-1], PC3.get(("pf24",))])
                    PC3[("ps_t",)] = m_ev
                m_xld = dma(PF(25), x_in[b, t0 + i * 128:t0 + (i + 1) * 128,
                                        :],
                            waits=[PC3.get(("pf25",))])
                m_o = vtt(PF(25), PF(25), PF(24), ALU.add,
                          waits=[m_ev, m_xld])
                PC3[("out", i)] = m_o
                PC3[("pf24",)] = m_o
                m_out_last = dma(
                    AP(out_t.tensor, (b * L + t0 + i * 128) * D,
                       [[D, 128], [1, D]]),
                    PF(25), waits=[m_o])
                PC3[("pf25",)] = m_out_last
        m_phaseC_done = m_out_last

    with (
        nc.semaphore("sem_q") as sem_q,
        nc.semaphore("sem_t") as sem_t,
        nc.semaphore("sem_a") as sem_a,
        nc.semaphore("sem_v") as sem_v,
        nc.semaphore("sem_g") as sem_g,
        nc.Block() as block,
    ):
        S.render(block, {"q": sem_q, "t": sem_t, "a": sem_a,
                         "v": sem_v, "g": sem_g})
    return nc


def kernel(**inputs):
    x = np.ascontiguousarray(inputs["L0_attn_x"], dtype=np.float32)
    pat_dt, pat_du, yd = host_tables()
    lane_h_full = np.array([l % 16 for l in range(NL)])
    lane_s_full = np.array([l // 16 for l in range(NL)])
    alog = np.asarray(inputs["L0_attn_A_log"], dtype=np.float32)
    alog_lane = alog[lane_h_full, lane_s_full]
    t_alog = alog_lane.reshape(NT, 128).T.copy()
    idf = np.eye(128, dtype=np.float32)
    idb = np.eye(128, dtype=np.float32).astype(bf)

    common = dict(
        norm1_w=np.asarray(inputs["L0_attn_norm1_w"], np.float32),
        norm2_w=np.asarray(inputs["L0_attn_norm2_w"], np.float32),
        dw_kernel=np.asarray(inputs["L0_attn_dw_kernel"], np.float32),
        dw_bias=np.asarray(inputs["L0_attn_dw_bias"], np.float32),
        pw_w=np.asarray(inputs["L0_attn_pw_w"], np.float32),
        pw_b=np.asarray(inputs["L0_attn_pw_b"], np.float32),
        xp_w=np.asarray(inputs["L0_attn_x_proj_w"], np.float32),
        dtp_w=np.asarray(inputs["L0_attn_dt_proj_w"], np.float32),
        dtp_b=np.asarray(inputs["L0_attn_dt_proj_b"], np.float32),
        out_w=np.asarray(inputs["L0_attn_out_proj_w"], np.float32),
        out_b=np.asarray(inputs["L0_attn_out_proj_b"], np.float32),
        gate_w=np.asarray(inputs["L0_attn_gate_w"], np.float32),
        up_w=np.asarray(inputs["L0_attn_up_w"], np.float32),
        down_w=np.asarray(inputs["L0_attn_down_w"], np.float32),
        ffn_w=np.asarray(inputs["L0_ffn_weight"], np.float32),
        ffn_b=np.asarray(inputs["L0_ffn_bias"], np.float32),
        t_pdt=pat_dt, t_pdu=pat_du, t_yd=yd, t_alog=t_alog,
        t_idb=idb, t_idf=idf,
    )
    in_maps = []
    for core in range(8):
        m = dict(common)
        m["x"] = x[core * BC:(core + 1) * BC]
        in_maps.append(m)

    nc = build()
    res = run_bass_kernel_spmd(nc, in_maps, core_ids=list(range(8)))
    return np.concatenate([r["out"] for r in res.results], axis=0)


# version 10 (final)
# speedup vs baseline: 1.0532x; 1.0051x over previous
import numpy as np
import ml_dtypes
import concourse.bass as bass
import concourse.mybir as mybir
from concourse.ap import AP
from concourse.bass_utils import run_bass_kernel_spmd

F32 = mybir.dt.float32
BF16 = mybir.dt.bfloat16
AF = mybir.ActivationFunctionType
ALU = mybir.AluOpType
bf = ml_dtypes.bfloat16

BC, L, D = 2, 2048, 1024
H, DH, DS, DI = 16, 64, 48, 2048
NL, NT = H * DS, 6
DH_SUB, TC = 4, 512
NDHG, NTC = DH // DH_SUB, L // TC
CC = DH_SUB * TC
TA, NSUB = 512, L // 512
DT_BLK, DU_BLK = 128 * TC, 64 * TC
RND = 12582912.0
DBG = False
LN_EPS, RMS_EPS = 1e-5, 1e-6

LANE_H = np.array([l % 16 for l in range(128)])
ENGINES = ("q", "t", "a", "v", "g")
_ATTR = {"q": "sync", "t": "tensor", "a": "scalar", "v": "vector",
         "g": "gpsimd"}


class Sched:
    def __init__(self):
        self.ops = []
        self.cnt = {e: 0 for e in ENGINES}

    def op(self, eng, fn, waits=(), inc=1):
        ws = []
        for w in waits:
            if w is None:
                continue
            if isinstance(w, (list, tuple)) and w and isinstance(w[0], tuple):
                ws.extend(x for x in w if x is not None)
            else:
                ws.append(w)
        for (we, wv) in ws:
            if we == eng:
                assert wv <= self.cnt[eng], "self-wait on future op"
        self.ops.append((eng, fn, inc, ws))
        self.cnt[eng] += inc
        return (eng, self.cnt[eng])

    def dma(self, eng, fn, waits=()):
        return self.op(eng, fn, waits, inc=16)

    def render(self, block, sems):
        wvals = {e: set() for e in ENGINES}
        for (_, _, _, waits) in self.ops:
            for (we, wv) in waits:
                wvals[we].add(wv)
        for eng in ENGINES:
            bvals = sorted(wvals[eng]) if eng in ("q", "g") else []

            def body(o, eng=eng, bvals=bvals):
                waited = {e: 0 for e in ENGINES}
                cum = 0
                bi = 0
                selfw = 0
                for (e, fn, inc, waits) in self.ops:
                    if e != eng:
                        continue
                    while bi < len(bvals) and bvals[bi] <= cum:
                        bi += 1
                    if bi > 0 and bvals[bi - 1] > selfw:
                        o.wait_ge(sems[eng], bvals[bi - 1])
                        selfw = bvals[bi - 1]
                    for (we, wv) in waits:
                        if wv > waited[we]:
                            o.wait_ge(sems[we], wv)
                            waited[we] = wv
                    fn(o).then_inc(sems[eng], inc)
                    cum += inc
                if eng == "q":
                    for e2 in ENGINES:
                        if e2 != "q" and self.cnt[e2] > waited[e2]:
                            o.wait_ge(sems[e2], self.cnt[e2])
                    o.wait_ge(sems["q"], self.cnt["q"])
            getattr(block, _ATTR[eng])(body)


def host_tables():
    pat_dt = np.zeros((DH_SUB, 64, 128), np.float32)
    pat_du = np.zeros((DH_SUB, 32, 128), np.float32)
    for d in range(DH_SUB):
        p_ = d // 2
        for k in range(64):
            gg = p_ * 64 + k
            if gg // 32 == d:
                pat_dt[d, k, :] = (LANE_H == (gg % 32) // 2)
        for k in range(32):
            gg = p_ * 32 + k
            if gg // 16 == d:
                pat_du[d, k, :] = (LANE_H == gg % 16)
    yd = np.zeros((128, 16), np.float32)
    for ml in range(128):
        yd[ml, LANE_H[ml]] = 1.0
    pat_dt = pat_dt.transpose(1, 0, 2).reshape(64, DH_SUB * 128)
    pat_du = pat_du.transpose(1, 0, 2).reshape(32, DH_SUB * 128)
    return pat_dt.astype(bf), pat_du.astype(bf), yd.astype(bf)


def build():
    nc = bass.Bass("TRN2", target_bir_lowering=False)
    S = Sched()

    ein = lambda n, s, dt=F32: nc.dram_tensor(n, s, dt, kind="ExternalInput").ap()
    x_in = ein("x", [BC, L, D])
    norm1_w = ein("norm1_w", [D])
    norm2_w = ein("norm2_w", [D])
    dw_kernel = ein("dw_kernel", [D, 1, 4])
    dw_bias = ein("dw_bias", [D])
    pw_w = ein("pw_w", [D, D])
    pw_b = ein("pw_b", [D])
    xp_w = ein("xp_w", [160, DH])
    dtp_w = ein("dtp_w", [DH, 64])
    dtp_b = ein("dtp_b", [64])
    out_w = ein("out_w", [D, D])
    out_b = ein("out_b", [D])
    gate_w = ein("gate_w", [DI, D])
    up_w = ein("up_w", [DI, D])
    down_w = ein("down_w", [D, DI])
    ffn_w = ein("ffn_w", [D, D])
    ffn_b = ein("ffn_b", [D])
    t_pdt = ein("t_pdt", [64, DH_SUB * 128], BF16)
    t_pdu = ein("t_pdu", [32, DH_SUB * 128], BF16)
    t_yd = ein("t_yd", [128, 16], BF16)
    t_alog = ein("t_alog", [128, NT])
    t_idb = ein("t_idb", [128, 128], BF16)
    t_idf = ein("t_idf", [128, 128])

    out_t = nc.dram_tensor("out", [BC, L, D], F32, kind="ExternalOutput").ap()

    dint = lambda n, s, dt: nc.dram_tensor(n, s, dt, kind="Internal").ap()
    g_dt = dint("g_dt", [BC, NDHG, NTC, DT_BLK], BF16)
    g_du = dint("g_du", [BC, NDHG, NTC, DU_BLK], BF16)
    g_bt = dint("g_bt", [BC, NL, L], BF16)
    g_ct = dint("g_ct", [BC, NL, L], BF16)
    g_y = dint("g_y", [BC, D, L], BF16)
    g_xn = dint("g_xn", [BC, D, L], F32)
    g_xq = dint("g_xq", [BC, D, L], BF16)
    g_xsc = dint("g_xsc", [BC, L], F32)
    g_x1 = dint("g_x1", [BC, D, L], F32)
    g_h2 = dint("g_h2", [BC, D, L], BF16)
    g_mid = dint("g_mid", [BC, DI, L], BF16)
    g_pwT = dint("g_pwT", [D, D], BF16)
    g_opT = dint("g_opT", [D, D], BF16)
    g_gwT = dint("g_gwT", [D, DI], BF16)
    g_uwT = dint("g_uwT", [D, DI], BF16)
    g_dnT = dint("g_dnT", [DI, D], BF16)
    g_fqT = dint("g_fqT", [D, D], BF16)
    if DBG:
        g_dbg_s = dint("g_dbg_s", [BC, D, L], F32)
        g_dbg_g = dint("g_dbg_g", [BC, D, L], F32)

    PF_N, PB_N = 27, 40
    pf = nc.alloc_sbuf_tensor("pf", [128, PF_N * 1024], F32)
    pb = nc.alloc_sbuf_tensor("pb", [128, PB_N * 1024], BF16)

    def PF(i, c0=0, n=1024, r0=0, rows=128):
        return AP(pf, r0 * (PF_N * 1024) + i * 1024 + c0,
                  [[PF_N * 1024, rows], [1, n]])

    def PB(i, c0=0, n=1024, r0=0, rows=128):
        return AP(pb, r0 * (PB_N * 1024) + i * 1024 + c0,
                  [[PB_N * 1024, rows], [1, n]])

    sbt = lambda n, s, dt=F32: nc.alloc_sbuf_tensor(n, s, dt)
    idb_s = sbt("idb", [128, 128], BF16)
    idf_s = sbt("idf", [128, 128])
    colA = sbt("colA", [128, 15])
    colB = sbt("colB", [128, 9])
    kcol = sbt("kcol", [128, 32])
    c_pwb = sbt("c_pwb", [128, 8])
    c_n1 = sbt("c_n1", [128, 8])
    c_n2 = sbt("c_n2", [128, 8])
    c_ob = sbt("c_ob", [128, 8])
    c_fb = sbt("c_fb", [128, 8])
    c_ws = sbt("c_ws", [128, 8])
    c_dtb = sbt("c_dtb", [128, 1])
    c_dwb = sbt("c_dwb", [128, 8])
    xwT_s = sbt("xwT", [128, 160], BF16)
    dwT_s = sbt("dwT", [128, 64], BF16)
    s_dtx = sbt("s_dtx", [128, 8 * TA], BF16)
    s_bc = sbt("s_bc", [48, 2 * TA], BF16)
    s_acol = sbt("s_acol", [128, NT])
    s_pd = sbt("s_pd", [128, DH_SUB * 128], BF16)
    s_yd = sbt("s_yd", [128, 16], BF16)
    s_rd = sbt("s_rd", [128, 2 * TC], BF16)
    s_carry = sbt("s_carry", [128, NT * DH_SUB])
    s_tmp = sbt("s_tmp", [128, DH_SUB])
    s_yev = sbt("s_yev", [16, 2048], BF16)
    halo_b = sbt("halo", [128, 24])
    prepQ = sbt("prepQ", [128, 4])
    ones_c = sbt("ones_c", [128, 128], BF16)
    e0_s = sbt("e0_s", [128, 128], BF16)

    ps_mm = nc.alloc_psum_tensor("ps_mm", [128, 2048], F32)
    ps_t = nc.alloc_psum_tensor("ps_t", [128, 512], F32)
    ps_r = nc.alloc_psum_tensor("ps_r", [128, 512], F32)
    ps_y = nc.alloc_psum_tensor("ps_y", [128, 512], F32)
    ps_tb = nc.alloc_psum_tensor("ps_tb", [128, 512], BF16)

    def dma(dst, src, waits=(), slow=False):
        if slow:
            return S.dma("q", lambda o, d=dst, s=src: o.dma_start(
                out=d, in_=s, allow_slow_non_contiguous=True), waits)
        return S.dma("q", lambda o, d=dst, s=src: o.dma_start(out=d, in_=s),
                     waits)

    def gdma(dst, src, waits=()):
        return S.dma("g", lambda o, d=dst, s=src: o.dma_start(out=d, in_=s),
                     waits)

    def vcopy(dst, src, waits=()):
        return S.op("v", lambda o, d=dst, s=src: o.tensor_copy(out=d, in_=s),
                    waits)

    def vtt(dst, a0, a1, op, waits=()):
        return S.op("v", lambda o, d=dst, x=a0, y=a1, op=op:
                    o.tensor_tensor(out=d, in0=x, in1=y, op=op), waits)

    def gtt(dst, a0, a1, op, waits=()):
        return S.op("g", lambda o, d=dst, x=a0, y=a1, op=op:
                    o.tensor_tensor(out=d, in0=x, in1=y, op=op), waits)

    def vts(dst, src, s1, s2=None, op0=ALU.mult, op1=None, waits=()):
        if s2 is None:
            return S.op("v", lambda o, d=dst, s=src, s1=s1, op0=op0:
                        o.tensor_scalar(out=d, in0=s, scalar1=s1,
                                        scalar2=None, op0=op0), waits)
        return S.op("v", lambda o, d=dst, s=src, s1=s1, s2=s2, op0=op0,
                    op1=op1: o.tensor_scalar(out=d, in0=s, scalar1=s1,
                                             scalar2=s2, op0=op0, op1=op1),
                    waits)

    def vstt(dst, in0, scal, in1, op0, op1, waits=()):
        return S.op("v", lambda o, d=dst, i0=in0, sc=scal, i1=in1, op0=op0,
                    op1=op1: o.scalar_tensor_tensor(
                        out=d, in0=i0, scalar=sc, in1=i1, op0=op0, op1=op1),
                    waits)

    def vred(dst, src, op=ALU.add, absv=False, waits=()):
        if op == ALU.add:
            return S.op("v", lambda o, d=dst, s=src, ab=absv: o.reduce_sum(
                out=d, in_=s, axis=mybir.AxisListType.X,
                apply_absolute_value=ab), waits)
        return S.op("v", lambda o, d=dst, s=src, ab=absv: o.reduce_max(
            out=d, in_=s, axis=mybir.AxisListType.X,
            apply_absolute_value=ab), waits)

    def vrecip(dst, src, waits=()):
        return S.op("v", lambda o, d=dst, s=src: o.reciprocal(out=d, in_=s),
                    waits)

    def vmemset(dst, val, waits=()):
        return S.op("v", lambda o, d=dst, vv=val: o.memset(d, vv), waits)

    def vscan(dst, d0, d1, waits=()):
        return S.op("v", lambda o, d=dst, a0=d0, a1=d1: o.tensor_tensor_scan(
            out=d, data0=a0, data1=a1, initial=0.0, op0=ALU.mult,
            op1=ALU.add), waits)

    def act(dst, src, func, bias=0.0, scale=1.0, waits=()):
        return S.op("a", lambda o, d=dst, s=src, f=func, b=bias, sc=scale:
                    o.activation(out=d, in_=s, func=f, bias=b, scale=sc),
                    waits)

    def mm(out, lhsT, rhs, start, stop, waits=(), skip=False):
        return S.op("t", lambda o, ot=out, lh=lhsT, rh=rhs, st=start,
                    sp=stop, sk=skip: o.matmul(out=ot, lhsT=lh, rhs=rh,
                                               start=st, stop=sp,
                                               skip_group_check=sk), waits)

    def tT(out, src, ident, waits=()):
        return S.op("t", lambda o, ot=out, s=src, idn=ident:
                    o.transpose(out=ot, in_=s, identity=idn), waits)

    class Banks:
        def __init__(self):
            self.last = [None] * 4
            self.i = 0

        def get(self):
            b_ = self.i
            self.i = (b_ + 1) % 4
            return b_, ps_mm.ap()[:, b_ * 512:(b_ + 1) * 512], self.last[b_]

        def done(self, b_, m):
            self.last[b_] = m

        def drain(self):
            return [m for m in self.last if m is not None]

    BK = Banks()

    def load_dcol(dst_t, src_vec, waits=()):
        return dma(AP(dst_t, 0, [[dst_t.shape[1], 128], [1, 8]]),
                   AP(src_vec.tensor, 0, [[1, 128], [128, 8]]),
                   waits=waits, slow=True)

    m_idb = dma(idb_s.ap(), t_idb)
    m_idf = dma(idf_s.ap(), t_idf)
    m_pdt = dma(AP(s_pd, 0, [[DH_SUB * 128, 64], [1, DH_SUB * 128]]), t_pdt)
    m_pdu = dma(AP(s_pd, 64 * DH_SUB * 128, [[DH_SUB * 128, 32],
                                             [1, DH_SUB * 128]]), t_pdu)
    m_yd = dma(s_yd.ap(), t_yd)
    m_al = dma(s_acol.ap(), t_alog)
    m_ax = act(s_acol.ap(), s_acol.ap(), AF.Exp, waits=[m_al])
    m_acol = vts(s_acol.ap(), s_acol.ap(), -1.0, waits=[m_ax])
    m_on1 = vmemset(ones_c.ap(), 1.0)
    m_e0a = vmemset(e0_s.ap(), 0.0)
    m_on2 = vmemset(e0_s.ap()[0:1, :], 1.0, waits=[m_e0a])

    m_n1 = load_dcol(c_n1, norm1_w)
    m_n2 = load_dcol(c_n2, norm2_w)
    m_ob = load_dcol(c_ob, out_b)
    m_fb = load_dcol(c_fb, ffn_b)
    m_dwb = load_dcol(c_dwb, dw_bias)
    m_pbv = load_dcol(colB, pw_b)
    m_dtb = dma(AP(c_dtb, 0, [[1, 64], [1, 1]]),
                AP(dtp_b.tensor, 0, [[1, 64], [1, 1]]))
    m_kc = dma(AP(kcol, 0, [[32, 128], [4, 8], [1, 4]]),
               AP(dw_kernel.tensor, 0, [[4, 128], [512, 8], [1, 4]]))
    m_kf = None
    for i in range(8):
        m_kf = vts(kcol.ap()[:, 4 * i:4 * i + 4],
                   kcol.ap()[:, 4 * i:4 * i + 4],
                   c_n1.ap()[:, i:i + 1], waits=[m_kc, m_n1, m_kf])

    ps_tb_last = [None]
    pb_ev_last = [None, None]
    prep_v_last = [None]

    def prep_T(w_ap, gdst, O, K, quant=False, ws_col=None):
        mstore, mload = None, None
        for ot in range(O // 128):
            md = dma(PF(25, 0, K), w_ap[ot * 128:(ot + 1) * 128, 0:K],
                     waits=[mload])
            src = md
            if quant:
                m1 = vred(prepQ.ap()[:, 0:1], PF(25, 0, K), op=ALU.max,
                          absv=True, waits=[md])
                m2 = vrecip(prepQ.ap()[:, 1:2], prepQ.ap()[:, 0:1],
                            waits=[m1])
                m3 = vts(prepQ.ap()[:, 2:3], prepQ.ap()[:, 1:2], 127.0,
                         waits=[m2])
                S.op("v", lambda o, ot=ot, ws=ws_col: o.tensor_scalar(
                    out=ws.ap()[:, ot:ot + 1], in0=prepQ.ap()[:, 0:1],
                    scalar1=1.0 / 127.0, scalar2=None, op0=ALU.mult),
                    waits=[m1])
                m5 = vts(PF(25, 0, K), PF(25, 0, K), prepQ.ap()[:, 2:3],
                         waits=[m3])
                src = vts(PF(25, 0, K), PF(25, 0, K), RND, RND, ALU.add,
                          ALU.subtract, waits=[m5])
            mc = vcopy(PB(24, 0, K), PF(25, 0, K), waits=[src])
            mload = mc
            prep_v_last[0] = mc
            for g4 in range(K // 512):
                mts_ = []
                for j in range(4):
                    kt = g4 * 4 + j
                    mts_.append(tT(ps_tb.ap()[:, j * 128:(j + 1) * 128],
                                   PB(24, kt * 128, 128), idb_s.ap(),
                                   waits=[mc, m_idb,
                                          mts_[-1] if mts_
                                          else ps_tb_last[0]]))
                ev = g4 % 2
                me = act(PB(24, K + ev * 512, 512), ps_tb.ap()[:, 0:512],
                         AF.Copy, waits=[mts_[-1], pb_ev_last[ev]])
                ps_tb_last[0] = me
                for j in range(4):
                    kt = g4 * 4 + j
                    mstore = dma(AP(gdst.tensor,
                                    gdst.offset + kt * 128 * O + ot * 128,
                                    [[O, 128], [1, 128]]),
                                 PB(24, K + ev * 512 + j * 128, 128),
                                 waits=[me] if j == 0 else [])
                pb_ev_last[ev] = mstore
        return mstore

    m_pw = prep_T(pw_w, g_pwT, D, D)
    m_opw = prep_T(out_w, g_opT, D, D)
    m_gw = prep_T(gate_w, g_gwT, DI, D)
    m_uw = prep_T(up_w, g_uwT, DI, D)
    m_dn = prep_T(down_w, g_dnT, D, DI)
    m_fq = prep_T(ffn_w, g_fqT, D, D, quant=True, ws_col=c_ws)

    m0 = dma(PF(25, 0, 64), xp_w[0:128, :], waits=[m_fq])
    m1 = dma(PF(26, 0, 64, rows=32), xp_w[128:160, :])
    m2 = vcopy(PB(24, 0, 64), PF(25, 0, 64), waits=[m0, m_fq])
    m3 = vcopy(PB(26, 0, 64, rows=32), PF(26, 0, 64, rows=32), waits=[m1])
    m4 = tT(ps_tb.ap()[0:64, 0:128], PB(24, 0, 64), idb_s.ap(),
            waits=[m2, ps_tb_last[0]])
    m5 = act(AP(xwT_s, 0, [[160, 64], [1, 128]]), ps_tb.ap()[0:64, 0:128], AF.Copy,
             waits=[m4])
    m6 = tT(ps_tb.ap()[0:64, 128:160], PB(26, 0, 64, rows=32),
            idb_s.ap()[0:32, 0:32], waits=[m3, m5, ps_tb_last[0]])
    m7 = act(AP(xwT_s, 128, [[160, 64], [1, 32]]), ps_tb.ap()[0:64, 128:160], AF.Copy,
             waits=[m6])
    m8 = dma(PF(25, 0, 64, rows=64), dtp_w, waits=[m2])
    m9 = vcopy(PB(24, 0, 64, rows=64), PF(25, 0, 64, rows=64),
               waits=[m8, m3, m4])
    m10 = tT(ps_tb.ap()[0:64, 256:320], PB(24, 0, 64, rows=64),
             idb_s.ap()[0:64, 0:64], waits=[m9, m7, ps_tb_last[0]])
    prep_v_last[0] = m9
    m_dwT0 = act(AP(dwT_s, 0, [[64, 64], [1, 64]]),
                 ps_tb.ap()[0:64, 256:320], AF.Copy, waits=[m10])
    m_xwC = vcopy(AP(xwT_s, 64 * 160, [[160, 64], [1, 160]]),
                  AP(xwT_s, 0, [[160, 64], [1, 160]]), waits=[m5, m7])
    m_dwT = vcopy(AP(dwT_s, 64 * 64, [[64, 64], [1, 64]]),
                  AP(dwT_s, 0, [[64, 64], [1, 64]]), waits=[m_dwT0, m_xwC])
    ps_tb_last[0] = m_dwT0

    def load_pwT(waits):
        ms = []
        for kt in range(8):
            ms.append(dma(PB(29 + kt, 0, 1024),
                          AP(g_pwT.tensor, kt * 128 * D,
                             [[D, 128], [1, 1024]]), waits=waits))
        return ms

    m_pwT = load_pwT([m_pw, m_dwT])
    m_dwbc = vcopy(PB(28, 0, 8), c_dwb.ap()[:, 0:8], waits=[m_dwb, m9])
    m_bias = None
    for ot in range(8):
        mmsl = None
        for kt in range(8):
            mmsl = mm(ps_y.ap()[:, 0:1], PB(29 + kt, ot * 128, 128),
                      PB(28, kt, 1), start=(kt == 0), stop=(kt == 7),
                      waits=[m_pwT[kt], m_dwbc, m_bias, mmsl])
        m_bias = act(c_pwb.ap()[:, ot:ot + 1], ps_y.ap()[:, 0:1], AF.Copy,
                     waits=[mmsl])
    m_pwb = vtt(c_pwb.ap()[:, 0:8], c_pwb.ap()[:, 0:8], colB.ap()[:, 0:8],
                ALU.add, waits=[m_pbv, m_bias])

    m_phaseC_done = None

    for b in range(BC):
        if b > 0:
            m_pwT = load_pwT([m_phaseC_done])
        P = {}
        m_stage_last = None
        for sub in range(NSUB):
            t0 = sub * TA
            mx = []
            for i in range(4):
                mx.append(dma(PF(i),
                              x_in[b, t0 + i * 128:t0 + (i + 1) * 128, :],
                              waits=[P.get(("xn", i)), m_phaseC_done]))
            mxn = []
            for i in range(4):
                ms1 = vred(colA.ap()[:, 0:1], PF(i),
                           waits=[mx[i], P.get(("cA",))])
                msq = act(PF(8), PF(i), AF.Square,
                          waits=[mx[i], P.get(("sq",))])
                ms2 = vred(colA.ap()[:, 1:2], PF(8), waits=[msq])
                m_mu = vts(colA.ap()[:, 2:3], colA.ap()[:, 0:1], 1.0 / D,
                           waits=[ms1])
                m_m2 = vtt(colA.ap()[:, 3:4], colA.ap()[:, 2:3],
                           colA.ap()[:, 2:3], ALU.mult, waits=[m_mu])
                m_var = vstt(colA.ap()[:, 4:5], colA.ap()[:, 1:2], 1.0 / D,
                             colA.ap()[:, 3:4], ALU.mult, ALU.subtract,
                             waits=[ms2, m_m2])
                m_ep = vts(colA.ap()[:, 4:5], colA.ap()[:, 4:5], 1.0,
                           LN_EPS, ALU.mult, ALU.add, waits=[m_var])
                m_ln = act(colA.ap()[:, 5:6], colA.ap()[:, 4:5], AF.Ln,
                           waits=[m_ep])
                m_sc = vts(colA.ap()[:, 5:6], colA.ap()[:, 5:6], -0.5,
                           waits=[m_ln])
                m_rs = act(colA.ap()[:, 6:7], colA.ap()[:, 5:6], AF.Exp,
                           waits=[m_sc])
                mxn.append(vts(PF(i), PF(i), colA.ap()[:, 2:3],
                               colA.ap()[:, 6:7], ALU.subtract, ALU.mult,
                               waits=[m_rs, msq]))
                P[("sq",)] = msq
                P[("cA",)] = mxn[-1]
            m_xq = []
            for i in range(4):
                ma = vred(colA.ap()[:, 8:9], PF(i), op=ALU.max, absv=True,
                          waits=[mxn[i], P.get(("cB",))])
                mr = vrecip(colA.ap()[:, 9:10], colA.ap()[:, 8:9],
                            waits=[ma])
                mi = vts(colA.ap()[:, 10:11], colA.ap()[:, 9:10], 127.0,
                         waits=[mr])
                msc = vts(colA.ap()[:, 11:12], colA.ap()[:, 8:9],
                          1.0 / 127.0, waits=[ma])
                mdsc = dma(AP(g_xsc.tensor, b * L + t0 + i * 128,
                              [[1, 128], [1, 1]]),
                           colA.ap()[:, 11:12], waits=[msc])
                P[("cB",)] = mdsc
                mq1 = vts(PF(8), PF(i), colA.ap()[:, 10:11],
                          waits=[mi, P[("sq",)]])
                mq2 = vts(PF(8), PF(8), RND, RND, ALU.add, ALU.subtract,
                          waits=[mq1])
                m_xq.append(vcopy(PB(i), PF(8),
                                  waits=[mq2, P.get(("xq", i))]))
                P[("sq",)] = m_xq[-1]
                P[("xq", i)] = m_xq[-1]
            m_v = []
            for i in range(4):
                msq = act(PF(8), PF(i), AF.Square, waits=[mxn[i],
                                                          P[("sq",)]])
                ms = vred(colA.ap()[:, 12:13], PF(8),
                          waits=[msq, P.get(("cC",))])
                m_ep = vts(colA.ap()[:, 12:13], colA.ap()[:, 12:13],
                           1.0 / D, RMS_EPS, ALU.mult, ALU.add, waits=[ms])
                m_ln = act(colA.ap()[:, 13:14], colA.ap()[:, 12:13], AF.Ln,
                           waits=[m_ep])
                m_sc = vts(colA.ap()[:, 13:14], colA.ap()[:, 13:14], -0.5,
                           waits=[m_ln])
                m_rs = act(colA.ap()[:, 14:15], colA.ap()[:, 13:14], AF.Exp,
                           waits=[m_sc])
                m_v.append(vts(PF(4 + i), PF(i), colA.ap()[:, 14:15],
                               waits=[m_rs, P.get(("v", i))]))
                P[("sq",)] = msq
                P[("v", i)] = m_v[-1]
                P[("cC",)] = m_v[-1]
            for dt_ in range(8):
                mts = []
                for i in range(4):
                    mts.append(tT(ps_t.ap()[:, i * 128:(i + 1) * 128],
                                  PF(4 + i, dt_ * 128, 128), idf_s.ap(),
                                  waits=[m_v[i], m_idf, P.get(("ps_t",)),
                                         mts[-1] if mts else None]))
                m_ev = act(PF(11 + dt_, 3, 512), ps_t.ap()[:, 0:512],
                           AF.Copy, waits=[mts[-1], P.get(("haloS", dt_)),
                                           P.get(("dtS", dt_))])
                P[("ps_t",)] = m_ev
                if sub == 0:
                    m_hw = vmemset(PF(11 + dt_, 0, 3), 0.0, waits=[m_ev])
                else:
                    m_hw = vcopy(PF(11 + dt_, 0, 3),
                                 halo_b.ap()[:, dt_ * 3:dt_ * 3 + 3],
                                 waits=[m_ev, P.get(("haloS", dt_))])
                mc = vts(PF(19 + dt_, 0, 512), PF(11 + dt_, 0, 512),
                         kcol.ap()[:, dt_ * 4:dt_ * 4 + 1],
                         waits=[m_ev, m_hw, m_kf, P.get(("xh", dt_)),
                                prep_v_last[0] if (b == 0 and sub == 0
                                                   and dt_ >= 6) else None])
                for k in range(1, 4):
                    if k == 3:
                        dst = PB(4 + dt_ // 2, (dt_ % 2) * 512, 512)
                        w_x = P.get(("ucb", dt_))
                    else:
                        dst = PF(19 + dt_, 0, 512)
                        w_x = None
                    mc = vstt(dst, PF(11 + dt_, k, 512),
                              kcol.ap()[:, dt_ * 4 + k:dt_ * 4 + k + 1],
                              PF(19 + dt_, 0, 512), ALU.mult, ALU.add,
                              waits=[mc, w_x])
                P[("ucb", dt_)] = mc
                P[("conv", dt_)] = mc
                m_hs = vcopy(halo_b.ap()[:, dt_ * 3:dt_ * 3 + 3],
                             PF(11 + dt_, TA, 3), waits=[mc])
                P[("haloS", dt_)] = m_hs
                mts2 = []
                for i in range(4):
                    mts2.append(tT(ps_t.ap()[:, i * 128:(i + 1) * 128],
                                   PF(i, dt_ * 128, 128), idf_s.ap(),
                                   waits=[mxn[i], P[("ps_t",)],
                                          mts2[-1] if mts2 else None]))
                m_ev2 = act(PF(8, 0, 512), ps_t.ap()[:, 0:512], AF.Copy,
                            waits=[mts2[-1], P[("sq",)]])
                P[("ps_t",)] = m_ev2
                m_st = dma(AP(g_xn.tensor, (b * D + dt_ * 128) * L + t0,
                              [[L, 128], [1, 512]]), PF(8, 0, 512),
                           waits=[m_ev2])
                P[("sq",)] = m_st
                P[("xn", 0)] = P[("xn", 1)] = P[("xn", 2)] = P[("xn", 3)] \
                    = m_st if dt_ == 7 else P.get(("xn", 0))
                mts3 = []
                for i in range(4):
                    mts3.append(tT(ps_tb.ap()[:, i * 128:(i + 1) * 128],
                                   PB(i, dt_ * 128, 128), idb_s.ap(),
                                   waits=[m_xq[i], ps_tb_last[0],
                                          mts3[-1] if mts3 else None]))
                m_ev3 = act(PB(38 + dt_ % 2, 0, 512), ps_tb.ap()[:, 0:512],
                            AF.Copy, waits=[mts3[-1],
                                            P.get(("xqS", dt_ % 2))])
                ps_tb_last[0] = m_ev3
                m_sq = dma(AP(g_xq.tensor, (b * D + dt_ * 128) * L + t0,
                              [[L, 128], [1, 512]]),
                           PB(38 + dt_ % 2, 0, 512), waits=[m_ev3])
                P[("xqS", dt_ % 2)] = m_sq
                if dt_ == 7:
                    for i in range(4):
                        P[("xq", i)] = m_sq
            for o8 in range(8):
                b_, bank, bfree = BK.get()
                mmsl = None
                for kt in range(8):
                    mmsl = mm(bank, PB(29 + kt, o8 * 128, 128),
                              PB(4 + kt // 2, (kt % 2) * 512, 512),
                              start=(kt == 0), stop=(kt == 7),
                              waits=[m_pwT[kt], P[("conv", kt)], bfree,
                                     mmsl])
                m_xh = act(PF(19 + o8, 0, 512), bank, AF.Identity,
                           bias=c_pwb.ap()[:, o8:o8 + 1],
                           waits=[mmsl, m_pwb, P[("conv", o8)]])
                BK.done(b_, m_xh)
                P[("xh", o8)] = m_xh
                m_xhb = vcopy(PB(8 + o8 // 2, (o8 % 2) * 512, 512),
                              PF(19 + o8, 0, 512),
                              waits=[m_xh, P.get(("xhb", o8))])
                P[("xhb", o8)] = m_xhb
            m_dtp = {}
            for h_ in range(16):
                hp, hl = h_ // 2, h_ % 2
                rhs = PB(8 + hp // 2, (hp % 2) * 512, 512, r0=64 * hl,
                         rows=64)
                w_rhs = P[("xhb", hp)]
                b1, bank1, bf1 = BK.get()
                mdtx = mm(AP(ps_mm, bank1.offset + (64 * hl) * 2048,
                             [[2048, 64], [1, 512]]),
                          AP(xwT_s, (64 * hl) * 160, [[160, 64], [1, 64]]),
                          rhs, True, True, waits=[w_rhs, m_xwC, bf1])
                m_dtxe = act(AP(s_dtx, (64 * hl) * (8 * TA) + hp * TA,
                                [[8 * TA, 64], [1, TA]]),
                             AP(ps_mm, bank1.offset + (64 * hl) * 2048,
                                [[2048, 64], [1, 512]]),
                             AF.Copy,
                             waits=[mdtx, P.get(("dtx", h_))])
                BK.done(b1, m_dtxe)
                P[("dtx", h_)] = m_dtxe
                b2, bank2, bf2 = BK.get()
                mB = mm(AP(ps_mm, bank2.offset + (64 * hl) * 2048,
                           [[2048, 48], [1, 512]]),
                        AP(xwT_s, (64 * hl) * 160 + 64, [[160, 64], [1, 48]]),
                        rhs, True, True, waits=[w_rhs, m_xwC, bf2])
                m_Be = act(AP(s_bc, (0) * TA if hl == 0 else 0, [[2 * TA, 48], [1, TA]])
                           if hl == 0 else AP(s_bc, 0, [[2 * TA, 48], [1, TA]]),
                           AP(ps_mm, bank2.offset + (64 * hl) * 2048,
                              [[2048, 48], [1, 512]]),
                           AF.Copy, waits=[mB, P.get(("bcS", 0))])
                BK.done(b2, m_Be)
                m_Bs = dma(AP(g_bt.tensor, (b * NL + h_) * L + t0,
                              [[16 * L, 48], [1, 512]]),
                           s_bc.ap()[:, 0:TA], waits=[m_Be])
                P[("bcS", 0)] = m_Bs
                b3, bank3, bf3 = BK.get()
                mC = mm(AP(ps_mm, bank3.offset + (64 * hl) * 2048,
                           [[2048, 48], [1, 512]]),
                        AP(xwT_s, (64 * hl) * 160 + 112, [[160, 64], [1, 48]]),
                        rhs, True, True, waits=[w_rhs, m_xwC, bf3])
                m_Ce = act(AP(s_bc, TA, [[2 * TA, 48], [1, TA]]),
                           AP(ps_mm, bank3.offset + (64 * hl) * 2048,
                              [[2048, 48], [1, 512]]),
                           AF.Copy, waits=[mC, P.get(("bcS2", 0))])
                BK.done(b3, m_Ce)
                m_Cs = dma(AP(g_ct.tensor, (b * NL + h_) * L + t0,
                              [[16 * L, 48], [1, 512]]),
                           s_bc.ap()[:, TA:2 * TA], waits=[m_Ce])
                P[("bcS2", 0)] = m_Cs
                m_stage_last = m_Cs
                b4, bank4, bf4 = BK.get()
                mdt = mm(AP(ps_mm, bank4.offset + (64 * hl) * 2048,
                            [[2048, 64], [1, 512]]),
                         AP(dwT_s, (64 * hl) * 64, [[64, 64], [1, 64]]),
                         AP(s_dtx, (64 * hl) * (8 * TA) + hp * TA,
                            [[8 * TA, 64], [1, TA]]), True, True,
                         waits=[m_dtxe, m_dwT, bf4])
                m_dte = act(PF(11 + hp, 0, 512, r0=64 * hl, rows=64),
                            AP(ps_mm, bank4.offset + (64 * hl) * 2048,
                               [[2048, 64], [1, 512]]),
                            AF.Identity,
                            bias=c_dtb.ap()[0:64, 0:1],
                            waits=[mdt, m_dtb, P[("conv", hp)],
                                   P[("haloS", hp)], P.get(("dtS", hp))])
                BK.done(b4, m_dte)
                m_dtp[h_] = m_dte
            for j in range(8):
                wj = [m_dtp[2 * j], m_dtp[2 * j + 1]]
                mab = act(PF(8, 0, 512), PF(11 + j, 0, 512), AF.Abs,
                          waits=wj + [P[("sq",)]])
                mex = act(PF(8, 0, 512), PF(8, 0, 512), AF.Exp, scale=-1.0,
                          waits=[mab])
                mln = act(PF(8, 0, 512), PF(8, 0, 512), AF.Ln, bias=1.0,
                          waits=[mex])
                mre = act(PF(9, 0, 512), PF(11 + j, 0, 512), AF.Relu,
                          waits=wj + [P.get(("sp9",))])
                mdt2 = vtt(PF(11 + j, 0, 512), PF(9, 0, 512), PF(8, 0, 512),
                           ALU.add, waits=[mln, mre] + wj)
                P[("sq",)] = mdt2
                P[("sp9",)] = mdt2
                mhi = vcopy(PB(12 + j // 2, (j % 2) * 512, 512),
                            PF(11 + j, 0, 512),
                            waits=[mdt2, P.get(("dthi", j))])
                mlo1 = vtt(PF(9, 0, 512), PF(11 + j, 0, 512),
                           PB(12 + j // 2, (j % 2) * 512, 512),
                           ALU.subtract, waits=[mhi])
                mlo = vcopy(PB(16 + j // 2, (j % 2) * 512, 512),
                            PF(9, 0, 512),
                            waits=[mlo1, P.get(("dtlo", j))])
                mdu = vtt(PB(20 + j // 2, (j % 2) * 512, 512),
                          PF(11 + j, 0, 512), PF(19 + j, 0, 512), ALU.mult,
                          waits=[mdt2, P[("xh", j)], P.get(("dtub", j))])
                P[("dthi", j)] = mhi
                P[("dtlo", j)] = mlo
                P[("dtub", j)] = mdu
                P[("sp9",)] = mlo
                P[("xh", j)] = mdu
                for hl in range(2):
                    h_ = 2 * j + hl
                    for x_ in range(2):
                        tile_i = (12 if x_ == 0 else 16) + j // 2
                        src_ap = AP(pb, (64 * hl) * (PB_N * 1024)
                                    + tile_i * 1024 + (j % 2) * 512,
                                    [[PB_N * 1024, 64], [1, TC]])
                        m_s = dma(
                            AP(g_dt.tensor,
                               (b * NDHG * NTC + sub) * DT_BLK
                               + (h_ * 2 + x_) * TC,
                               [[NTC * DT_BLK, NDHG], [32 * TC, DH_SUB],
                                [1, TC]]),
                            src_ap, waits=[mhi if x_ == 0 else mlo])
                        m_stage_last = m_s
                    src_ap = AP(pb, (64 * hl) * (PB_N * 1024)
                                + (20 + j // 2) * 1024 + (j % 2) * 512,
                                [[PB_N * 1024, 64], [1, TC]])
                    m_s = dma(
                        AP(g_du.tensor,
                           (b * NDHG * NTC + sub) * DU_BLK + h_ * TC,
                           [[NTC * DU_BLK, NDHG], [16 * TC, DH_SUB],
                            [1, TC]]),
                        src_ap, waits=[mdu])
                    m_stage_last = m_s
                P[("dtS", j)] = m_stage_last

        m_bt = dma(AP(pb, 0, [[PB_N * 1024, 128], [1, NT * L]]),
                   AP(g_bt.tensor, b * NL * L,
                      [[L, 128], [128 * L, NT], [1, L]]),
                   waits=[m_stage_last, P[("xq", 0)], P[("bcS", 0)],
                          P[("bcS2", 0)], P[("xqS", 0)], P[("xqS", 1)]])
        m_ct = dma(AP(pb, 12 * 1024, [[PB_N * 1024, 128], [1, NT * L]]),
                   AP(g_ct.tensor, b * NL * L,
                      [[L, 128], [128 * L, NT], [1, L]]),
                   waits=[m_stage_last])
        chunks = [(dhg, tc_) for dhg in range(NDHG) for tc_ in range(NTC)]
        NCH = len(chunks)
        m_repmm = [None] * NCH
        m_dumm = [None] * NCH
        m_duev = [None] * NCH
        m_exps = [None] * NCH
        m_rdt = [None] * NCH
        m_rdu = [None] * NCH
        m_abuf = {}
        m_abuf0 = [None]

        def a_tile(Pp_, ti_):
            if ti_ == 0:
                return 0
            return 2 * ti_ if Pp_ == 0 else 10 + 2 * ti_
        m_wfree = [None, None]
        m_wlast = [None, None]
        m_mfree = [None, None]
        scan_st = {"h": None, "yevl": None}
        m_carry = [None] * NT
        m_ydma = [None] * 4
        YQ = [(ps_y, 0), (ps_y, 32), (ps_r, 0), (ps_r, 32)]

        def chunk_prep(k):
            dhg, tc_ = chunks[k]
            Pp = k % 2
            blk = (b * NDHG + dhg) * NTC + tc_
            w_pre = [m_bt, m_ct] if k == 0 else []
            mrdt_n = []
            for p_ in range(2):
                mrdt_n.append(dma(
                    AP(s_rd, p_ * TC, [[2 * TC, 64], [1, TC]]),
                    AP(g_dt.tensor, blk * DT_BLK + p_ * 64 * TC,
                       [[TC, 64], [1, TC]]),
                    waits=[m_repmm[k - 1] if k else None,
                           m_rdt[k - 1] if k else None] + w_pre))
            m_rdt[k] = mrdt_n[-1]
            mrdu_n = []
            for p_ in range(2):
                mrdu_n.append(dma(
                    AP(s_rd, 64 * 2 * TC + p_ * TC, [[2 * TC, 32], [1, TC]]),
                    AP(g_du.tensor, blk * DU_BLK + p_ * 32 * TC,
                       [[TC, 32], [1, TC]]),
                    waits=[m_dumm[k - 1] if k else None,
                           m_rdu[k - 1] if k else None]))
            m_rdu[k] = mrdu_n[-1]
            mmx = m_duev[k - 1] if k else None
            for d in range(DH_SUB):
                p_ = d // 2
                mmx = mm(ps_mm.ap()[:, d * TC:(d + 1) * TC],
                         AP(s_pd, d * 128, [[DH_SUB * 128, 64], [1, 128]]),
                         AP(s_rd, p_ * TC, [[2 * TC, 64], [1, TC]]),
                         True, True, waits=[mrdt_n[p_], m_pdt, mmx])
            m_repmm[k] = mmx
            m_exp = []
            for ti in range(NT):
                m_exp.append(act(PF(a_tile(Pp, ti), 0, 2048), ps_mm.ap(),
                                 AF.Exp, scale=s_acol.ap()[:, ti:ti + 1],
                                 waits=[m_repmm[k], m_acol,
                                        m_abuf0[0] if ti == 0
                                        else m_abuf.get((Pp, ti))]))
            m_exps[k] = m_exp
            mmx = m_exp[-1]
            for d in range(DH_SUB):
                p_ = d // 2
                mmx = mm(ps_mm.ap()[:, d * TC:(d + 1) * TC],
                         AP(s_pd, 64 * DH_SUB * 128 + d * 128,
                            [[DH_SUB * 128, 32], [1, 128]]),
                         AP(s_rd, 64 * 2 * TC + p_ * TC,
                            [[2 * TC, 32], [1, TC]]),
                         True, True, waits=[mrdu_n[p_], m_pdu, mmx])
            m_dumm[k] = mmx
            m_duev[k] = act(PB(24 + 2 * Pp, 0, 2048), ps_mm.ap(), AF.Copy,
                            waits=[m_dumm[k], m_wlast[Pp]])

        h_free = [None, None]

        def chunk_tiles(k, prep_next=None):
            dhg, tc_ = chunks[k]
            Pp = k % 2
            du_i = 24 + 2 * Pp
            m_ws = [None] * NT

            def emit_w(ti):
                pp = ti % 2
                m_ws[ti] = gtt(
                    PB(28 + pp * 2, 0, 2048).rearrange(
                        "p (d t) -> p d t", d=DH_SUB),
                    PB(du_i, 0, 2048).rearrange("p (d t) -> p d t",
                                                d=DH_SUB),
                    AP(pb, ti * L + tc_ * TC,
                       [[PB_N * 1024, 128], [0, DH_SUB], [1, TC]]),
                    ALU.mult,
                    waits=[m_duev[k], m_bt, m_wfree[pp]])
                m_wlast[Pp] = m_ws[ti]

            emit_w(0)
            emit_w(1)
            for ti in range(NT):
                if ti == 1 and prep_next is not None:
                    prep_next()
                pp = ti % 2
                a_i = a_tile(Pp, ti)
                w_i = 28 + pp * 2
                m_i = 32 + pp * 2
                h_i = 22 + 2 * (ti % 2)
                a_seg = AP(pf, a_i * 1024, [[PF_N * 1024, 128],
                                            [TC, DH_SUB]])
                w_seg = AP(pb, w_i * 1024, [[PB_N * 1024, 128],
                                            [TC, DH_SUB]])
                m_w = m_ws[ti]
                cs = s_carry.ap()[:, ti * DH_SUB:(ti + 1) * DH_SUB]
                if tc_ == 0:
                    m_fx = vmemset(a_seg, 0.0, waits=[m_exps[k][ti]])
                else:
                    m_t1 = vtt(s_tmp.ap(), a_seg, cs, ALU.mult,
                               waits=[m_exps[k][ti], m_carry[ti]])
                    m_t2 = m_t1
                    for d in range(DH_SUB):
                        wcol = AP(pb, w_i * 1024 + d * TC,
                                  [[PB_N * 1024, 128], [1, 1]])
                        m_t2 = vtt(wcol, wcol,
                                   s_tmp.ap()[:, d:d + 1], ALU.add,
                                   waits=[m_t2, m_w if d == 0 else None])
                    m_fx = vmemset(a_seg, 0.0, waits=[m_t2])
                m_scan = vscan(PF(h_i, 0, 2048), PF(a_i, 0, 2048),
                               PB(w_i, 0, 2048),
                               waits=[m_fx, m_w, h_free[ti % 2]])
                if ti == 0:
                    m_abuf0[0] = m_scan
                else:
                    m_abuf[(Pp, ti)] = m_scan
                m_wfree[pp] = m_scan
                if ti + 2 < NT:
                    emit_w(ti + 2)
                m_cw = vcopy(cs, AP(pf, h_i * 1024 + TC - 1,
                                    [[PF_N * 1024, 128], [TC, DH_SUB]]),
                             waits=[m_scan])
                m_carry[ti] = m_cw
                m_m = vtt(
                    PB(m_i, 0, 2048).rearrange("p (d t) -> p d t",
                                               d=DH_SUB),
                    PF(h_i, 0, 2048).rearrange("p (d t) -> p d t",
                                               d=DH_SUB),
                    AP(pb, 12 * 1024 + ti * L + tc_ * TC,
                       [[PB_N * 1024, 128], [0, DH_SUB], [1, TC]]),
                    ALU.mult,
                    waits=[m_scan, m_ct, m_mfree[pp]])
                h_free[ti % 2] = [m_m, m_cw]
                for qt in range(4):
                    pq, r0 = YQ[qt]
                    m_yq = mm(AP(pq, r0 * 512, [[512, 16], [1, 512]]),
                              s_yd.ap(), PB(m_i, qt * 512, 512),
                              start=(ti == 0), stop=(ti == NT - 1),
                              waits=[m_m, m_yd,
                                     scan_st["yevl"] if ti == 0 else None],
                              skip=True)
                    m_mfree[pp] = m_yq
                if ti == NT - 1:
                    m_evs = []
                    for qt in range(4):
                        pq, r0 = YQ[qt]
                        m_evs.append(act(
                            AP(s_yev, qt * 512, [[2048, 16], [1, 512]]),
                            AP(pq, r0 * 512, [[512, 16], [1, 512]]),
                            AF.Copy,
                            waits=[m_mfree[pp], m_ydma[qt]]))
                    scan_st["yevl"] = m_evs[-1]
                    for qt in range(4):
                        m_ydma[qt] = dma(
                            AP(g_y.tensor,
                               (b * D + dhg * DH_SUB + qt) * L + tc_ * TC,
                               [[64 * L, 16], [1, TC]]),
                            AP(s_yev, qt * 512, [[2048, 16], [1, 512]]),
                            waits=[m_evs[qt]])

        chunk_prep(0)
        for k in range(NCH):
            chunk_tiles(k, prep_next=(lambda kk=k + 1: chunk_prep(kk))
                        if k + 1 < NCH else None)
        m_scan_done = [m_ydma[qt] for qt in range(4)]
        m_hbuf = [x for x in h_free[0] + h_free[1] if x is not None]

        m_opTs = []
        for kt in range(8):
            m_opTs.append(dma(PB(kt, 0, 1024),
                              AP(g_opT.tensor, kt * 128 * D,
                                 [[D, 128], [1, 1024]]),
                              waits=[m_hbuf, m_wlast[0], m_wlast[1],
                                     m_mfree[0], m_mfree[1], m_duev[-1],
                                     scan_st["yevl"]] + m_scan_done))
        PC = {}
        m_h2S = {}
        m_x1S = {}
        for sub in range(NSUB):
            t0 = sub * TA
            m_yb = []
            for j in range(8):
                m_yb.append(dma(PB(32 + j // 2, (j % 2) * 512, 512),
                                AP(g_y.tensor, (b * D + j * 128) * L + t0,
                                   [[L, 128], [1, 512]]),
                                waits=[PC.get(("yb", j)),
                                       PC.get(("ybs", j))]
                                + ([m_mfree[0], m_mfree[1], m_hbuf]
                                   + m_scan_done if sub == 0 else [])))
            m_mm_last = []
            for o8 in range(8):
                b_, bank, bfree = BK.get()
                mmsl = None
                for kt in range(8):
                    mmsl = mm(bank, PB(kt, o8 * 128, 128),
                              PB(32 + kt // 2, (kt % 2) * 512, 512),
                              start=(kt == 0), stop=(kt == 7),
                              waits=[m_opTs[kt], m_yb[kt], bfree, mmsl])
                m_mm_last.append(mmsl)
                m_oe = act(PF(8 + o8, 0, 512), bank, AF.Identity,
                           bias=c_ob.ap()[:, o8:o8 + 1],
                           waits=[mmsl, m_ob, PC.get(("sq", o8))])
                BK.done(b_, m_oe)
                m_xl = dma(PF(16 + o8, 0, 512),
                           AP(g_xn.tensor, (b * D + o8 * 128) * L + t0,
                              [[L, 128], [1, 512]]),
                           waits=[PC.get(("xn", o8))]
                           + ([m_hbuf] if sub == 0 else []))
                m_x1 = vtt(PF(o8, 0, 512), PF(8 + o8, 0, 512),
                           PF(16 + o8, 0, 512), ALU.add,
                           waits=[m_oe, m_xl, PC.get(("x1s", o8)),
                                  PC.get(("x1", o8))])
                PC[("x1", o8)] = m_x1
                PC[("xn", o8)] = m_x1
                m_x1s = dma(AP(g_x1.tensor, (b * D + o8 * 128) * L + t0,
                               [[L, 128], [1, 512]]),
                            PF(o8, 0, 512), waits=[m_x1])
                m_x1S[(sub, o8)] = m_x1s
                PC[("x1s", o8)] = m_x1s
            for o8 in range(8):
                m_sq = act(PF(8 + o8, 0, 512), PF(o8, 0, 512), AF.Square,
                           waits=[PC[("x1", o8)], PC.get(("sq", o8))])
                PC[("sq", o8)] = m_sq
                m_hi = vcopy(PB(36 + o8 // 2, (o8 % 2) * 512, 512),
                             PF(8 + o8, 0, 512),
                             waits=[m_sq, PC.get(("sqh", o8))])
                m_lo1 = vtt(PF(8 + o8, 0, 512), PF(8 + o8, 0, 512),
                            PB(36 + o8 // 2, (o8 % 2) * 512, 512),
                            ALU.subtract, waits=[m_hi])
                m_lo = vcopy(PB(32 + o8 // 2, (o8 % 2) * 512, 512),
                             PF(8 + o8, 0, 512),
                             waits=[m_lo1, m_mm_last[-1],
                                    PC.get(("sql", o8))])
                PC[("sqh", o8)] = m_hi
                PC[("sql", o8)] = m_lo
                PC[("sq", o8)] = m_lo
            mmsl = PC.get(("msrow",))
            first = True
            for j in range(8):
                for which in range(2):
                    t_i = (36 if which == 0 else 32) + j // 2
                    mmsl = mm(ps_y.ap(), ones_c.ap(),
                              PB(t_i, (j % 2) * 512, 512),
                              start=first, stop=(j == 7 and which == 1),
                              waits=[PC[("sqh", j)] if which == 0
                                     else PC[("sql", j)],
                                     m_on1, mmsl])
                    first = False
            m_pre = vts(PF(24, 0, 512), ps_y.ap(), 1.0 / D, RMS_EPS,
                        ALU.mult, ALU.add,
                        waits=[mmsl, PC.get(("msrow2",)), PC.get(("rowb",))])
            m_ms = act(PF(24, 0, 512), PF(24, 0, 512), AF.Ln,
                       waits=[m_pre])
            PC[("msrow",)] = m_ms
            m_sc2 = vts(PF(24, 0, 512), PF(24, 0, 512), -0.5, waits=[m_ms])
            m_rep = act(PF(24, 0, 512), PF(24, 0, 512), AF.Exp,
                       waits=[m_sc2])
            PC[("rowb",)] = m_rep
            PC[("msrow2",)] = m_rep
            for o8 in range(8):
                m_h2 = vstt(PB(32 + o8 // 2, (o8 % 2) * 512, 512),
                            PF(o8, 0, 512), c_n2.ap()[:, o8:o8 + 1],
                            PF(24, 0, 512), ALU.mult, ALU.mult,
                            waits=[PC[("x1", o8)], m_rep, m_n2])
                PC[("yb", o8)] = m_h2
                PC[("h2rep",)] = m_h2
                m_h2s = dma(AP(g_h2.tensor, (b * D + o8 * 128) * L + t0,
                               [[L, 128], [1, 512]]),
                            PB(32 + o8 // 2, (o8 % 2) * 512, 512),
                            waits=[m_h2])
                m_h2S[(sub, o8)] = m_h2s
                PC[("ybs", o8)] = m_h2s

        m_gws, m_uws = [], []
        for kt in range(8):
            m_gws.append(dma(
                AP(pb, (2 * kt) * 1024, [[PB_N * 1024, 128], [1, 2048]]),
                AP(g_gwT.tensor, kt * 128 * DI, [[DI, 128], [1, 2048]]),
                waits=[m_h2S[(NSUB - 1, 7)], m_opTs[kt]]))
        for kt in range(8):
            m_uws.append(dma(
                AP(pb, (16 + 2 * kt) * 1024, [[PB_N * 1024, 128], [1, 2048]]),
                AP(g_uwT.tensor, kt * 128 * DI, [[DI, 128], [1, 2048]]),
                waits=[m_h2S[(NSUB - 1, 7)]]))
        PC2 = {}
        m_midS = {}
        for sub in range(NSUB):
            t0 = sub * TA
            m_h2l = []
            for j in range(8):
                m_h2l.append(dma(PB(32 + j // 2, (j % 2) * 512, 512),
                                 AP(g_h2.tensor, (b * D + j * 128) * L + t0,
                                    [[L, 128], [1, 512]]),
                                 waits=[PC2.get(("h2", j)),
                                        m_h2S[(sub, j)]] +
                                 ([m_uws[-1]] if sub == 0 else [])))
                PC2[("h2", j)] = m_h2l[-1]
            for o16 in range(16):
                bg, bankg, bfg = BK.get()
                mmsl = None
                for kt in range(8):
                    mmsl = mm(bankg,
                              AP(pb, (2 * kt) * 1024 + o16 * 128,
                                 [[PB_N * 1024, 128], [1, 128]]),
                              PB(32 + kt // 2, (kt % 2) * 512, 512),
                              start=(kt == 0), stop=(kt == 7),
                              waits=[m_gws[kt], m_h2l[kt], bfg, mmsl])
                m_sg = act(PB(36 + o16 % 2, 0, 512), bankg, AF.Silu,
                           waits=[mmsl, PC2.get(("sgb", o16 % 2))])
                BK.done(bg, m_sg)
                bu, banku, bfu = BK.get()
                mmsl = None
                for kt in range(8):
                    mmsl = mm(banku,
                              AP(pb, (16 + 2 * kt) * 1024 + o16 * 128,
                                 [[PB_N * 1024, 128], [1, 128]]),
                              PB(32 + kt // 2, (kt % 2) * 512, 512),
                              start=(kt == 0), stop=(kt == 7),
                              waits=[m_uws[kt], m_h2l[kt], bfu, mmsl])
                m_ue = act(PB(38 + o16 % 2, 0, 512), banku, AF.Copy,
                           waits=[mmsl, PC2.get(("ueb", o16 % 2))])
                BK.done(bu, m_ue)
                m_mid = vtt(PB(36 + o16 % 2, 0, 512),
                            PB(36 + o16 % 2, 0, 512),
                            PB(38 + o16 % 2, 0, 512), ALU.mult,
                            waits=[m_sg, m_ue])
                PC2[("ueb", o16 % 2)] = m_mid
                m_ms = dma(AP(g_mid.tensor, (b * DI + o16 * 128) * L + t0,
                              [[L, 128], [1, 512]]),
                           PB(36 + o16 % 2, 0, 512), waits=[m_mid])
                PC2[("sgb", o16 % 2)] = m_ms
                m_midS[(sub, o16)] = m_ms

        m_dns, m_fqs = [], []
        for kt in range(16):
            m_dns.append(dma(PB(kt, 0, 1024),
                             AP(g_dnT.tensor, kt * 128 * D,
                                [[D, 128], [1, 1024]]),
                             waits=[m_midS[(NSUB - 1, 15)],
                                    (m_gws + m_uws)[kt]]))
        for kt in range(8):
            m_fqs.append(dma(PB(16 + kt, 0, 1024),
                             AP(g_fqT.tensor, kt * 128 * D,
                                [[D, 128], [1, 1024]]),
                             waits=[m_midS[(NSUB - 1, 15)],
                                    m_uws[kt]]))
        PC3 = {}
        m_out_last = None
        for sub in range(NSUB):
            t0 = sub * TA
            m_midl = []
            for j in range(16):
                m_midl.append(dma(PB(24 + j // 2, (j % 2) * 512, 512),
                                  AP(g_mid.tensor,
                                     (b * DI + j * 128) * L + t0,
                                     [[L, 128], [1, 512]]),
                                  waits=[PC3.get(("mid", j)),
                                         m_midS[(sub, j)]] +
                                  ([m_dns[-1], m_fqs[-1]] if sub == 0
                                   else [])))
                PC3[("mid", j)] = m_midl[-1]
            m_xql = []
            for j in range(8):
                m_xql.append(dma(PB(32 + j // 2, (j % 2) * 512, 512),
                                 AP(g_xq.tensor, (b * D + j * 128) * L + t0,
                                    [[L, 128], [1, 512]]),
                                 waits=[PC3.get(("xq", j))] +
                                 ([m_fqs[-1]] if sub == 0 else [])))
                PC3[("xq", j)] = m_xql[-1]
            if sub == 0:
                m_z36 = vmemset(PB(36, 0, 512), 0.0,
                                waits=[PC2.get(("sgb", 0)),
                                       PC2.get(("ueb", 0))])
                PC3[("z36",)] = m_z36
            m_xsl = dma(PF(26, 0, 512, rows=1),
                        AP(g_xsc.tensor, b * L + t0, [[TA, 1], [1, TA]]),
                        waits=[PC3.get(("xsc",)), PC.get(("rowb",))])
            m_xsb = vcopy(PB(36, 0, 512, rows=1), PF(26, 0, 512, rows=1),
                          waits=[m_xsl, PC3.get(("xscb",)),
                                 PC3.get(("z36",))])
            m_xrep = mm(ps_r.ap(), e0_s.ap(), PB(36, 0, 512),
                        True, True,
                        waits=[m_xsb, m_on2, PC3.get(("t2rep",))])
            PC3[("xsc",)] = m_xsb
            PC3[("xscb",)] = m_xrep
            for o8 in range(8):
                bd, bankd, bfd = BK.get()
                mmsl = None
                for kt in range(16):
                    mmsl = mm(bankd, PB(kt, o8 * 128, 128),
                              PB(24 + kt // 2, (kt % 2) * 512, 512),
                              start=(kt == 0), stop=(kt == 15),
                              waits=[m_dns[kt], m_midl[kt], bfd, mmsl])
                m_de = act(PF(8 + o8, 0, 512), bankd, AF.Copy,
                           waits=[mmsl, PC3.get(("pf8", o8))])
                BK.done(bd, m_de)
                m_x1l = dma(PF(16 + o8, 0, 512),
                            AP(g_x1.tensor, (b * D + o8 * 128) * L + t0,
                               [[L, 128], [1, 512]]),
                            waits=[PC3.get(("x1", o8)), m_x1S[(sub, o8)]])
                m_s = vtt(PF(o8, 0, 512), PF(8 + o8, 0, 512),
                          PF(16 + o8, 0, 512), ALU.add,
                          waits=[m_de, m_x1l, PC3.get(("s", o8))])
                PC3[("s", o8)] = m_s
                bq, bankq, bfq = BK.get()
                mmsl = None
                for kt in range(8):
                    mmsl = mm(bankq, PB(16 + kt, o8 * 128, 128),
                              PB(32 + kt // 2, (kt % 2) * 512, 512),
                              start=(kt == 0), stop=(kt == 7),
                              waits=[m_fqs[kt], m_xql[kt], bfq, mmsl])
                m_t1 = act(PF(16 + o8, 0, 512), bankq, AF.Identity,
                           scale=c_ws.ap()[:, o8:o8 + 1],
                           waits=[mmsl, m_s, m_x1l])
                BK.done(bq, m_t1)
                PC3[("x1", o8)] = m_t1
                m_t2 = vtt(PF(16 + o8, 0, 512), PF(16 + o8, 0, 512),
                           ps_r.ap(), ALU.mult,
                           waits=[m_t1, m_xrep])
                PC3[("t2rep",)] = m_t2
                m_g = act(PF(8 + o8, 0, 512), PF(16 + o8, 0, 512), AF.Silu,
                          bias=c_fb.ap()[:, o8:o8 + 1],
                          waits=[m_t2, m_fb, m_de])
                m_dbg = None
                if DBG:
                    m_d1 = dma(AP(g_dbg_s.tensor,
                                  (b * D + o8 * 128) * L + t0,
                                  [[L, 128], [1, 512]]),
                               PF(o8, 0, 512), waits=[m_s])
                    m_d2 = dma(AP(g_dbg_g.tensor,
                                  (b * D + o8 * 128) * L + t0,
                                  [[L, 128], [1, 512]]),
                               PF(8 + o8, 0, 512), waits=[m_g])
                    m_dbg = [m_d1, m_d2]
                m_sg2 = vtt(PF(o8, 0, 512), PF(o8, 0, 512),
                            PF(8 + o8, 0, 512), ALU.mult,
                            waits=[m_g, m_s, m_dbg])
                PC3[("pf8", o8)] = m_sg2
                PC3[("s", o8)] = m_sg2
            for i in range(4):
                mts = []
                for half in range(2):
                    for j in range(4):
                        o8 = half * 4 + j
                        mts.append(tT(ps_t.ap()[:, j * 128:(j + 1) * 128],
                                      PF(o8, i * 128, 128), idf_s.ap(),
                                      waits=[PC3[("s", o8)],
                                             PC3.get(("ps_t",)),
                                             mts[-1] if mts else None]))
                    m_ev = act(PF(24, half * 512, 512),
                               ps_t.ap()[:, 0:512], AF.Copy,
                               waits=[mts[-1], PC3.get(("pf24",))])
                    PC3[("ps_t",)] = m_ev
                m_xld = dma(PF(25), x_in[b, t0 + i * 128:t0 + (i + 1) * 128,
                                        :],
                            waits=[PC3.get(("pf25",))])
                m_o = vtt(PF(25), PF(25), PF(24), ALU.add,
                          waits=[m_ev, m_xld])
                PC3[("out", i)] = m_o
                PC3[("pf24",)] = m_o
                m_out_last = dma(
                    AP(out_t.tensor, (b * L + t0 + i * 128) * D,
                       [[D, 128], [1, D]]),
                    PF(25), waits=[m_o])
                PC3[("pf25",)] = m_out_last
        m_phaseC_done = m_out_last

    with (
        nc.semaphore("sem_q") as sem_q,
        nc.semaphore("sem_t") as sem_t,
        nc.semaphore("sem_a") as sem_a,
        nc.semaphore("sem_v") as sem_v,
        nc.semaphore("sem_g") as sem_g,
        nc.Block() as block,
    ):
        S.render(block, {"q": sem_q, "t": sem_t, "a": sem_a,
                         "v": sem_v, "g": sem_g})
    return nc


def kernel(**inputs):
    x = np.ascontiguousarray(inputs["L0_attn_x"], dtype=np.float32)
    pat_dt, pat_du, yd = host_tables()
    lane_h_full = np.array([l % 16 for l in range(NL)])
    lane_s_full = np.array([l // 16 for l in range(NL)])
    alog = np.asarray(inputs["L0_attn_A_log"], dtype=np.float32)
    alog_lane = alog[lane_h_full, lane_s_full]
    t_alog = alog_lane.reshape(NT, 128).T.copy()
    idf = np.eye(128, dtype=np.float32)
    idb = np.eye(128, dtype=np.float32).astype(bf)

    common = dict(
        norm1_w=np.asarray(inputs["L0_attn_norm1_w"], np.float32),
        norm2_w=np.asarray(inputs["L0_attn_norm2_w"], np.float32),
        dw_kernel=np.asarray(inputs["L0_attn_dw_kernel"], np.float32),
        dw_bias=np.asarray(inputs["L0_attn_dw_bias"], np.float32),
        pw_w=np.asarray(inputs["L0_attn_pw_w"], np.float32),
        pw_b=np.asarray(inputs["L0_attn_pw_b"], np.float32),
        xp_w=np.asarray(inputs["L0_attn_x_proj_w"], np.float32),
        dtp_w=np.asarray(inputs["L0_attn_dt_proj_w"], np.float32),
        dtp_b=np.asarray(inputs["L0_attn_dt_proj_b"], np.float32),
        out_w=np.asarray(inputs["L0_attn_out_proj_w"], np.float32),
        out_b=np.asarray(inputs["L0_attn_out_proj_b"], np.float32),
        gate_w=np.asarray(inputs["L0_attn_gate_w"], np.float32),
        up_w=np.asarray(inputs["L0_attn_up_w"], np.float32),
        down_w=np.asarray(inputs["L0_attn_down_w"], np.float32),
        ffn_w=np.asarray(inputs["L0_ffn_weight"], np.float32),
        ffn_b=np.asarray(inputs["L0_ffn_bias"], np.float32),
        t_pdt=pat_dt, t_pdu=pat_du, t_yd=yd, t_alog=t_alog,
        t_idb=idb, t_idf=idf,
    )
    in_maps = []
    for core in range(8):
        m = dict(common)
        m["x"] = x[core * BC:(core + 1) * BC]
        in_maps.append(m)

    nc = build()
    res = run_bass_kernel_spmd(nc, in_maps, core_ids=list(range(8)))
    return np.concatenate([r["out"] for r in res.results], axis=0)
